# version 07 (final)
# speedup vs baseline: 1.0706x; 1.0030x over previous
import contextlib
import os
os.environ.setdefault("NEURON_SCRATCHPAD_PAGE_SIZE", "1024")
import numpy as np
import concourse.bass as bass
import concourse.mybir as mybir
from concourse.bass_utils import run_bass_kernel_spmd

F32 = mybir.dt.float32
BF16 = mybir.dt.bfloat16
I32 = mybir.dt.int32
AF = mybir.ActivationFunctionType
ALU = mybir.AluOpType
AX = mybir.AxisListType

NCORE = 8
CH = 2048
T = 2048
FT = 512
NCH = CH // 128
NTC = T // FT
LN_EPS = 1e-5
RMS_EPS = 1.1920929e-07

ENGINES = ("tensor", "vector", "scalar", "gpsimd", "sync")



class _Op:
    __slots__ = ("eng", "emit", "deps", "inc", "post", "needed", "idx", "sched",
                 "dmasem", "dma_prev", "dma_post", "ndma", "coll")

    def __init__(self, eng, emit, deps, inc, sched):
        self.eng = eng
        self.emit = emit
        self.deps = deps
        self.inc = inc
        self.post = None
        self.needed = False
        self.idx = None
        self.sched = sched
        self.dmasem = None
        self.dma_prev = 0
        self.dma_post = 0
        self.ndma = 0


class Sched:
    drain = None
    """Dependency scheduler over raw Bass engines; one Sched per nc.Block.

    Buffer-key maps are shared across phases (cross-block deps become real
    semaphore waits). Emission is deferred until Prog.close() so that ops can
    be marked sync-needed after their phase is declared.
    """

    def __init__(self, nc, name, last_writer, readers_since, prog):
        self.nc = nc
        self.name = name
        self.prog = prog
        self.ops = []
        self.last_writer = last_writer
        self.readers_since = readers_since
        self.sems = prog.sems
        self.counts = prog.counts

    def op(self, eng, emit, reads=(), writes=(), deps=(), dma=False, ndma=1,
           coll=False):
        dd = list(deps)
        for b in reads:
            w = self.last_writer.get(b)
            if w is not None:
                dd.append(w)
        for b in writes:
            w = self.last_writer.get(b)
            if w is not None:
                dd.append(w)
            dd.extend(self.readers_since.get(b, ()))
        seen = set()
        uniq = []
        for d in dd:
            if d is not None and id(d) not in seen:
                seen.add(id(d))
                uniq.append(d)
        o = _Op(eng, emit, uniq, 16 * ndma if dma else 1, self)
        o.coll = coll
        if dma:
            pr = self.prog
            slot = pr.dma_slot[eng] % pr.NDMASEM
            pr.dma_slot[eng] += 1
            o.dmasem = pr.dma_sems[eng][slot]
            o.dma_prev = pr.dma_tot[eng][slot]
            o.ndma = ndma
            pr.dma_tot[eng][slot] += 16 * ndma
            o.dma_post = pr.dma_tot[eng][slot]
            prev_op = pr.dma_lastop[eng][slot]
            if prev_op is not None and prev_op not in uniq:
                uniq.append(prev_op)
            pr.dma_lastop[eng][slot] = o
        o.idx = len(self.ops)
        self.ops.append(o)
        for b in reads:
            self.readers_since.setdefault(b, []).append(o)
        for b in writes:
            self.last_writer[b] = o
            self.readers_since[b] = []
        return o


class Prog:

    NDMASEM = 8

    def __init__(self, nc):
        self.nc = nc
        self.es = contextlib.ExitStack()
        self.cur_es = self.es
        self.prefix = ""
        self.nphase = 0
        self.scheds = []
        self.last_writer = {}
        self.readers_since = {}
        self.sems = {e: self.es.enter_context(nc.semaphore(f"E_{e}"))
                     for e in ENGINES}
        self.counts = {e: 0 for e in ENGINES}
        self.bar_sem = self.es.enter_context(nc.semaphore("BAR"))
        self.nbar = 0
        self.dma_sems = {e: [self.es.enter_context(nc.semaphore(f"D{e}_{i}"))
                             for i in range(self.NDMASEM)] for e in ENGINES}
        self.dma_slot = {e: 0 for e in ENGINES}
        self.dma_tot = {e: [0] * self.NDMASEM for e in ENGINES}
        self.dma_lastop = {e: [None] * self.NDMASEM for e in ENGINES}

    def barrier(self):
        self.nbar += 1
        self.scheds.append(("barrier", self.nbar))

    def sb(self, name, p, f, dt=BF16, es=None, side=None):
        kw = {}
        if side:
            kw["side"] = side
        return (es or self.cur_es).enter_context(
            self.nc.sbuf_tensor(self.prefix + name, [p, f], dt, **kw))

    def psum(self, name, p, f, es=None):
        return (es or self.cur_es).enter_context(
            self.nc.psum_tensor(self.prefix + name, [p, f], F32))

    @contextlib.contextmanager
    def phase(self, tag=""):
        es = contextlib.ExitStack()
        s = Sched(self.nc, f"{self.nphase}{tag}", self.last_writer, self.readers_since, self)
        self.nphase += 1
        try:
            yield s, es
            self.scheds.append(s)
        finally:
            es.close()

    def _check_deadlock(self):
        done = set()
        for s in self.scheds:
            if isinstance(s, tuple):
                continue
            per_eng = {e: [o for o in s.ops if o.eng == e] for e in ENGINES}
            ptr = {e: 0 for e in ENGINES}
            progressed = True
            while progressed:
                progressed = False
                for e in ENGINES:
                    while ptr[e] < len(per_eng[e]):
                        o = per_eng[e][ptr[e]]
                        if all(id(d) in done for d in o.deps):
                            done.add(id(o))
                            ptr[e] += 1
                            progressed = True
                        else:
                            break
            stuck = [e for e in ENGINES if ptr[e] < len(per_eng[e])]
            if stuck:
                raise RuntimeError(
                    f"Sched[{s.name}] deadlock: engines {stuck} at op idxs "
                    f"{[per_eng[e][ptr[e]].idx for e in stuck]}")

    def emit_all(self):
        for s in self.scheds:
            if isinstance(s, tuple):
                continue
            last_per_eng = {}
            for o in s.ops:
                for d in o.deps:
                    d.needed = True
                if o.coll:
                    o.needed = True
                if o.dmasem is None:
                    last_per_eng[o.eng] = o
            for o in last_per_eng.values():
                o.needed = True
        for s in self.scheds:
            if isinstance(s, tuple):
                continue
            for o in s.ops:
                if o.needed and o.dmasem is None:
                    self.counts[o.eng] += o.inc
                    o.post = self.counts[o.eng]
            s.drain = dict(self.counts)
        self._check_deadlock()
        hw = {}
        for s in self.scheds:
            if isinstance(s, tuple):
                _, k = s
                with self.nc.Block() as block:
                    for e in ENGINES:
                        def bbody(eng, k=k):
                            eng.sem_inc(self.bar_sem, 1)
                            eng.wait_ge(self.bar_sem, 5 * k)
                        getattr(block, e)(bbody)
                continue
            per_eng = {e: [o for o in s.ops if o.eng == e] for e in ENGINES}

            def make_body(e, s=s, per_eng=per_eng, hw=hw):
                def body(eng):
                    used_dma = {}
                    for o in per_eng[e]:
                        for d in o.deps:
                            if d.dmasem is not None:
                                sem, val = d.dmasem, d.dma_post
                            else:
                                sem, val = d.sched.sems[d.eng], d.post
                            key = (e, id(sem))
                            if val > hw.get(key, 0):
                                eng.wait_ge(sem, val)
                                hw[key] = val
                        if o.dmasem is not None and o.dma_prev > 0:
                            key = (e, id(o.dmasem))
                            if o.dma_prev > hw.get(key, 0):
                                eng.wait_ge(o.dmasem, o.dma_prev)
                                hw[key] = o.dma_prev
                        last = o.emit(eng)
                        if o.dmasem is not None:
                            used_dma[id(o.dmasem)] = (o.dmasem, o.dma_post)
                            if isinstance(last, list):
                                assert o.ndma == len(last), \
                                    f"op idx {o.idx} on {e}: ndma mismatch"
                                for li in last:
                                    li.then_inc(o.dmasem, 16)
                            else:
                                assert o.ndma == 1
                                last.then_inc(o.dmasem, 16)
                        elif o.needed:
                            assert last is not None and not isinstance(last, list), \
                                f"op idx {o.idx} on {e} returned no/多 instruction"
                            last.then_inc(s.sems[o.eng], o.inc)
                    if any(o.dmasem is None for o in per_eng[e]) and s.drain[e] > 0:
                        key = (e, id(s.sems[e]))
                        if s.drain[e] > hw.get(key, 0):
                            eng.wait_ge(s.sems[e], s.drain[e])
                            hw[key] = s.drain[e]
                    for sem, val in used_dma.values():
                        key = (e, id(sem))
                        if val > hw.get(key, 0):
                            eng.wait_ge(sem, val)
                            hw[key] = val
                return body

            with self.nc.Block() as block:
                for e in ENGINES:
                    if per_eng[e]:
                        getattr(block, e)(make_body(e))

    def close(self):
        self.emit_all()
        self.es.close()



ALL_CONSTS = [LN_EPS, RMS_EPS, 1e-6, 1.0, 1.0 / CH, 1024.0, 0.5, -1.0]


def _ctx_setup(p, nc, io):
    ctx = {}
    ctx["cst"] = p.sb("cst", 128, len(ALL_CONSTS), F32, es=p.es)
    for i, v in enumerate(ALL_CONSTS):
        nc.const_aps.register(F32, v, ctx["cst"][:, i:i + 1])
    ctx["onescol"] = p.sb("onescol", 1, 128, F32, es=p.es)
    ctx["onescs"] = p.sb("onescs", 128, 1, F32, es=p.es)
    ctx["onescs_b"] = p.sb("onescs_b", 128, 1, es=p.es)
    ctx["sident"] = p.sb("sident", 128, 128, F32, es=p.es)
    ctx["onest"] = p.sb("onest", 128, T, es=p.es)
    with p.phase("ctx") as (s, es):
        s.op("sync", lambda e: e.dma_start(ctx["onescol"][:], io["ones_in"]), dma=True,
             writes=["onescol"])
        s.op("sync", lambda e: e.dma_start(ctx["onescs"][:], io["onesc_in"]), dma=True,
             writes=["onescs"])
        s.op("sync", lambda e: e.dma_start(ctx["sident"][:], io["ident_in"]), dma=True,
             writes=["sident"])
        s.op("vector", lambda e: e.memset(ctx["onescs_b"][:], 1.0), writes=["onescs_b"])
        s.op("vector", lambda e: e.memset(ctx["onest"][:], 1.0), writes=["onest"])
        for i, v in enumerate(ALL_CONSTS):
            s.op("vector", (lambda ii, vv: lambda e: e.memset(ctx["cst"][:, ii:ii + 1], vv))(i, v),
                 writes=[("cst", i)])
    return ctx



def build_l0_attn(dbg=False, env=None):
    if env is None:
        nc = bass.Bass("TRN2", target_bir_lowering=False)
        io = {}
        io["x_fm"] = nc.dram_tensor("x_fm", [CH, T], F32, kind="ExternalInput").ap()
        io["wqkvT"] = nc.dram_tensor("wqkvT", [CH, 3 * CH], F32, kind="ExternalInput").ap()
        io["wswT"] = nc.dram_tensor("wswT", [CH, 16384], F32, kind="ExternalInput").ap()
        io["woutT"] = nc.dram_tensor("woutT", [8192, CH], F32, kind="ExternalInput").ap()
        io["ones_in"] = nc.dram_tensor("ones_in", [1, 128], F32, kind="ExternalInput").ap()
        io["onesc_in"] = nc.dram_tensor("onesc_in", [128, 1], F32, kind="ExternalInput").ap()
        io["ident_in"] = nc.dram_tensor("ident_in", [128, 128], F32, kind="ExternalInput").ap()
        io["smask_in"] = nc.dram_tensor("smask_in", [128, 256], F32, kind="ExternalInput").ap()
        io["x1_fm"] = nc.dram_tensor("x1_fm", [CH, T], F32, kind="ExternalOutput").ap()
        for nm, shape, dt in [("q_sp", [CH, T], BF16), ("k_sp", [CH, T], BF16),
                              ("v_sp", [CH, T], BF16), ("usg_sp", [8192, T], BF16),
                              ("wout_bf", [8192, CH], BF16), ("rows_sp", [2, T], F32),
                              ("l0a_cc_in", [32, 128], F32),
                              ("l0a_cc_out", [NCORE * 32, 128], F32)]:
            io[nm] = nc.dram_tensor(nm, shape, dt, kind="Internal").ap()
        p = Prog(nc)
        ctx = _ctx_setup(p, nc, io)
    else:
        nc, p, io, ctx = env
    emit_l0_attn(nc, p, io, ctx, dbg=dbg)
    if env is None:
        p.close()
        return nc
    return None


def emit_l0_attn(nc, p, io, ctx, dbg=False):
    x_fm, wqkvT, wswT, woutT = io["x_fm"], io["wqkvT"], io["wswT"], io["woutT"]
    smask_in = io["smask_in"]
    x1_fm = io["x1_fm"]
    q_sp, k_sp, v_sp = io["q_sp"], io["k_sp"], io["v_sp"]
    usg_sp, wout_bf, rows_sp = io["usg_sp"], io["wout_bf"], io["rows_sp"]
    cc_in, cc_out = io["l0a_cc_in"], io["l0a_cc_out"]
    onescol, onescs, onescs_b = ctx["onescol"], ctx["onescs"], ctx["onescs_b"]
    sident, onest = ctx["sident"], ctx["onest"]

    dbg = False
    smask = p.sb("smask", 128, 256, F32)
    cbRQ = p.sb("cbRQ", 128, T)
    cbRK = p.sb("cbRK", 128, T)
    spre = p.sb("spre", 128, 32, F32)
    scols = p.sb("scols", 128, 32, F32)
    es_r1 = contextlib.ExitStack()
    A2 = p.sb("A2", 128, NCH * T, es=es_r1, side="right")
    es_r2 = contextlib.ExitStack()
    r_mean = p.sb("r_mean", 1, T, F32, es=es_r2, side="right")
    r_var = p.sb("r_var", 1, T, F32, es=es_r2, side="right")
    r_r1 = p.sb("r_r1", 1, T, F32, es=es_r2, side="right")
    r_tmp = p.sb("r_tmp", 1, T, F32, es=es_r2, side="right")
    cbA = p.sb("cbA", 128, T, es=es_r2, side="right")
    cbB = p.sb("cbB", 128, T, es=es_r2, side="right")

    with p.phase("cst") as (s, es):
        s.op("sync", lambda e: e.dma_start(smask[:], smask_in), dma=True,
             writes=["smask"])

    with p.phase("st") as (s, es):
        trb = [p.sb(f"p1t{i}", 128, T, es=es) for i in range(3)]
        tfq = [p.sb(f"p1q{i}", 128, T, es=es) for i in range(2)]
        ps_sum = p.psum("ps_sum", 1, T, es=es)
        ps_sq = p.psum("ps_sq", 1, T, es=es)
        for k in range(NCH):
            b = trb[k % 3]
            s.op("gpsimd",
                 (lambda kk, bb: lambda e: e.dma_start(
                     bb[:], x_fm[kk * 128:(kk + 1) * 128, :]))(k, b),
                 dma=True, writes=[("tr", k % 3)])
            s.op("tensor",
                 (lambda kk, bb: lambda e: [
                     e.matmul(ps_sum[:, tc * FT:(tc + 1) * FT], onescs_b[:, :],
                              bb[:, tc * FT:(tc + 1) * FT],
                              start=(kk == 0), stop=(kk == NCH - 1))
                     for tc in range(NTC)][-1])(k, b),
                 reads=[("tr", k % 3), "onescs_b"], writes=["ps_sum"])
            sq = tfq[k % 2]
            s.op("scalar",
                 (lambda bb, qq: lambda e: e.activation(qq[:], bb[:], AF.Square))(b, sq),
                 reads=[("tr", k % 3)], writes=[("sq", k % 2)])
            s.op("tensor",
                 (lambda kk, qq: lambda e: [
                     e.matmul(ps_sq[:, tc * FT:(tc + 1) * FT], onescs_b[:, :],
                              qq[:, tc * FT:(tc + 1) * FT],
                              start=(kk == 0), stop=(kk == NCH - 1))
                     for tc in range(NTC)][-1])(k, sq),
                 reads=[("sq", k % 2), "onescs_b"], writes=["ps_sq"])
        s.op("vector", lambda e: e.tensor_copy(r_mean[:], ps_sum[:1, :]),
             reads=["ps_sum"], writes=["r_mean"])
        s.op("vector", lambda e: e.tensor_copy(r_var[:], ps_sq[:1, :]),
             reads=["ps_sq"], writes=["r_var"])

    with p.phase("rw") as (s, es):
        psb = [p.psum(f"psb{i}", 128, FT, es=es) for i in range(2)]
        s.op("vector", lambda e: e.tensor_scalar(r_mean[:], r_mean[:], 1.0 / CH, None, ALU.mult),
             reads=["r_mean"], writes=["r_mean"])
        s.op("vector", lambda e: e.tensor_scalar(r_var[:], r_var[:], 1.0 / CH, None, ALU.mult),
             reads=["r_var"], writes=["r_var"])
        s.op("vector", lambda e: e.tensor_tensor(r_tmp[:], r_mean[:], r_mean[:], ALU.mult),
             reads=["r_mean"], writes=["r_tmp"])
        s.op("vector", lambda e: e.tensor_tensor(r_var[:], r_var[:], r_tmp[:], ALU.subtract),
             reads=["r_tmp", "r_var"], writes=["r_var"])
        s.op("scalar", lambda e: e.activation(r_tmp[:], r_var[:], AF.Ln, bias=LN_EPS),
             reads=["r_var"], writes=["r_tmp"])
        s.op("scalar", lambda e: e.activation(r_r1[:], r_tmp[:], AF.Exp, scale=-0.5),
             reads=["r_tmp"], writes=["r_r1"])
        s.op("vector", lambda e: e.tensor_tensor(r_tmp[:], r_r1[:], r_r1[:], ALU.mult),
             reads=["r_r1"], writes=["r_tmp"])
        s.op("vector", lambda e: e.tensor_tensor(r_tmp[:], r_tmp[:], r_var[:], ALU.mult),
             reads=["r_tmp", "r_var"], writes=["r_tmp"])
        s.op("scalar", lambda e: e.activation(r_tmp[:], r_tmp[:], AF.Ln, bias=RMS_EPS),
             reads=["r_tmp"], writes=["r_tmp"])
        s.op("scalar", lambda e: e.activation(r_tmp[:], r_tmp[:], AF.Exp, scale=-0.5),
             reads=["r_tmp"], writes=["r_tmp"])
        s.op("vector", lambda e: e.tensor_tensor(r_var[:], r_r1[:], r_tmp[:], ALU.mult),
             reads=["r_r1", "r_tmp"], writes=["r_var"])
        s.op("vector", lambda e: e.tensor_tensor(r_tmp[:], r_mean[:], r_var[:], ALU.mult),
             reads=["r_mean", "r_var"], writes=["r_tmp"])
        s.op("sync", lambda e: e.dma_start(rows_sp[0:1, :], r_mean[:]), dma=True,
             reads=["r_mean"])
        s.op("sync", lambda e: e.dma_start(rows_sp[1:2, :], r_r1[:]), dma=True,
             reads=["r_r1"])
        for tc in range(NTC):
            s.op("tensor",
                 (lambda t: lambda e: e.matmul(
                     psb[0][:], onescol[:1, :], r_var[:, t * FT:(t + 1) * FT],
                     start=True, stop=True))(tc),
                 reads=["r_var", "onescol"], writes=[("psb", 0)])
            s.op("vector",
                 (lambda t: lambda e: e.tensor_copy(cbA[:, t * FT:(t + 1) * FT], psb[0][:]))(tc),
                 reads=[("psb", 0)], writes=["cbA"])
            s.op("tensor",
                 (lambda t: lambda e: e.matmul(
                     psb[1][:], onescol[:1, :], r_tmp[:, t * FT:(t + 1) * FT],
                     start=True, stop=True))(tc),
                 reads=["r_tmp", "onescol"], writes=[("psb", 1)])
            s.op("vector",
                 (lambda t: lambda e: e.tensor_copy(cbB[:, t * FT:(t + 1) * FT], psb[1][:]))(tc),
                 reads=[("psb", 1)], writes=["cbB"])

    with p.phase("xr") as (s, es):
        trb = [p.sb(f"p1ct{i}", 128, T, es=es) for i in range(3)]
        for k in range(NCH):
            b = trb[k % 3]
            s.op("gpsimd",
                 (lambda kk, bb: lambda e: e.dma_start(
                     bb[:], x_fm[kk * 128:(kk + 1) * 128, :]))(k, b),
                 dma=True, writes=[("tr", k % 3)])
            s.op("vector",
                 (lambda bb: lambda e: e.tensor_tensor(bb[:], bb[:], cbA[:], ALU.mult))(b),
                 reads=[("tr", k % 3), "cbA"], writes=[("tr", k % 3)])
            s.op("vector",
                 (lambda kk, bb: lambda e: e.tensor_tensor(
                     A2[:, kk * T:(kk + 1) * T], bb[:], cbB[:], ALU.subtract))(k, b),
                 reads=[("tr", k % 3), "cbB"], writes=[("A2", k)])
        if dbg:
            s.op("sync", lambda e: [
                e.dma_start(dbg_xr[k * 128:(k + 1) * 128, :], A2[:, k * T:(k + 1) * T])
                for k in range(NCH)], dma=True, ndma=NCH,
                deps=[s.last_writer[("A2", k)] for k in range(NCH)])
    es_r2.close()

    with p.phase("qkv") as (s, es):
        wbuf = p.sb("wbuf", 128, 2 * NCH * FT, es=es)
        trb = [p.sb(f"p2t{i}", 128, T, es=es) for i in range(4)]
        tfq = [p.sb(f"p2q{i}", 128, T, es=es) for i in range(2)]
        sqacc_q = p.sb("sqacc_q", 128, T, F32, es=es)
        sqacc_k = p.sb("sqacc_k", 128, T, F32, es=es)
        psq = [p.psum(f"psq{i}", 128, FT, es=es) for i in range(4)]
        ps_r = p.psum("ps_r", 1, T, es=es)

        s.op("vector", lambda e: e.memset(sqacc_q[:], 0.0), writes=["accq"])
        s.op("vector", lambda e: e.memset(sqacc_k[:], 0.0), writes=["acck"])

        NMO = 3 * CH // 128
        NG = NMO // 4
        for grp in range(NG):
            half = grp % 2
            s.op("gpsimd",
                 (lambda g, h: lambda e: [
                     e.dma_start(
                         wbuf[:, h * NCH * FT + k * FT:h * NCH * FT + (k + 1) * FT],
                         wqkvT[k * 128:(k + 1) * 128, g * FT:(g + 1) * FT])
                     for k in range(NCH)])(grp, half),
                 dma=True, ndma=NCH, writes=[("wbuf", half)])
            for mo in range(4):
                mo_g = grp * 4 + mo
                for tc in range(NTC):
                    i = mo_g * NTC + tc
                    pt = psq[i % 4]
                    s.op("tensor",
                         (lambda h, m, t, pp: lambda e: [
                             e.matmul(pp[:],
                                      wbuf[:, h * NCH * FT + k * FT + m * 128:
                                           h * NCH * FT + k * FT + m * 128 + 128],
                                      A2[:, k * T + t * FT: k * T + t * FT + FT],
                                      start=(k == 0), stop=(k == NCH - 1))
                             for k in range(NCH)][-1])(half, mo, tc, pt),
                         reads=[("wbuf", half)], writes=[("psq", i % 4)])
                    dst = trb[mo_g % 4]
                    s.op("vector",
                         (lambda dd, t, pp: lambda e: e.tensor_copy(
                             dd[:, t * FT:(t + 1) * FT], pp[:]))(dst, tc, pt),
                         reads=[("psq", i % 4)], writes=[("chunk", mo_g % 4)])
                dst = trb[mo_g % 4]
                if mo_g < 32:
                    sq = tfq[mo_g % 2]
                    s.op("scalar",
                         (lambda dd, qq: lambda e: e.activation(qq[:], dd[:], AF.Square))(dst, sq),
                         reads=[("chunk", mo_g % 4)], writes=[("sqt", mo_g % 2)])
                    acc = sqacc_q if mo_g < 16 else sqacc_k
                    key = "accq" if mo_g < 16 else "acck"
                    s.op("gpsimd",
                         (lambda aa, qq: lambda e: e.tensor_tensor(
                             aa[:], aa[:], qq[:], ALU.add))(acc, sq),
                         reads=[("sqt", mo_g % 2)], writes=[key])
                tgt = (q_sp if mo_g < 16 else k_sp if mo_g < 32 else v_sp)
                roff = (mo_g % 16) * 128
                s.op("sync",
                     (lambda tt, rr, dd: lambda e: e.dma_start(
                         tt[rr:rr + 128, :], dd[:]))(tgt, roff, dst),
                     dma=True, reads=[("chunk", mo_g % 4)])

        r_qk = p.sb("r_qk", 1, T, F32, es=es)
        for which, acckey, acc, rrow in (("q", "accq", sqacc_q, r_qk),
                                         ("k", "acck", sqacc_k, r_qk)):
            s.op("tensor",
                 (lambda aa: lambda e: [
                     e.matmul(ps_r[:, tc * FT:(tc + 1) * FT], onescs[:, :],
                              aa[:, tc * FT:(tc + 1) * FT], start=True, stop=True)
                     for tc in range(NTC)][-1])(acc),
                 reads=[acckey, "onescs"], writes=["ps_r"])
            s.op("vector",
                 (lambda rr: lambda e: e.tensor_copy(rr[:], ps_r[:1, :]))(rrow),
                 reads=["ps_r"], writes=[("row", which)])
            s.op("scalar",
                 (lambda rr: lambda e: e.activation(rr[:], rr[:], AF.Ln,
                                                    bias=RMS_EPS, scale=1.0 / CH))(rrow),
                 reads=[("row", which)], writes=[("row", which)])
            s.op("scalar",
                 (lambda rr: lambda e: e.activation(rr[:], rr[:], AF.Exp, scale=-0.5))(rrow),
                 reads=[("row", which)], writes=[("row", which)])
            cb = cbRQ if which == "q" else cbRK
            cbkey = "cbRQ" if which == "q" else "cbRK"
            for tc in range(NTC):
                s.op("tensor",
                     (lambda t, rr: lambda e: e.matmul(
                         psq[0][:], onescol[:1, :], rr[:, t * FT:(t + 1) * FT],
                         start=True, stop=True))(tc, rrow),
                     reads=[("row", which), "onescol"], writes=[("psq", 0)])
                s.op("vector",
                     (lambda t, cc: lambda e: e.tensor_copy(cc[:, t * FT:(t + 1) * FT], psq[0][:]))(tc, cb),
                     reads=[("psq", 0)], writes=[cbkey])
    es_r1.close()
    es_a1 = contextlib.ExitStack()
    A1 = p.sb("A1", 128, NCH * T, es=es_a1)

    with p.phase("sums") as (s, es):
        kb = [p.sb(f"p3k{i}", 128, T, es=es) for i in range(2)]
        vb = [p.sb(f"p3v{i}", 128, T, es=es) for i in range(2)]
        wb = [p.sb(f"p3w{i}", 128, T, es=es) for i in range(2)]
        for jc in range(NCH):
            r = jc % 2
            s.op("sync",
                 (lambda j, bb: lambda e: e.dma_start(bb[:], k_sp[j * 128:(j + 1) * 128, :]))(jc, kb[r]),
                 dma=True, writes=[("kc", r)])
            s.op("sync",
                 (lambda j, bb: lambda e: e.dma_start(bb[:], v_sp[j * 128:(j + 1) * 128, :]))(jc, vb[r]),
                 dma=True, writes=[("vc", r)])
            s.op("vector",
                 (lambda bb, ww: lambda e: e.tensor_tensor(ww[:], bb[:], cbRK[:], ALU.mult))(kb[r], wb[r]),
                 reads=[("kc", r), "cbRK"], writes=[("wc", r)])
            s.op("scalar",
                 (lambda ww: lambda e: e.activation(ww[:], ww[:], AF.Exp))(wb[r]),
                 reads=[("wc", r)], writes=[("wc", r)])
            s.op("vector",
                 (lambda j, ww: lambda e: e.reduce_sum(
                     scols[:, 16 + j:16 + j + 1], ww[:], AX.X))(jc, wb[r]),
                 reads=[("wc", r)], writes=["scols"])
            s.op("vector",
                 (lambda bb, ww: lambda e: e.tensor_tensor(ww[:], ww[:], bb[:], ALU.mult))(vb[r], wb[r]),
                 reads=[("wc", r), ("vc", r)], writes=[("wc", r)])
            s.op("vector",
                 (lambda j, ww: lambda e: e.reduce_sum(
                     scols[:, j:j + 1], ww[:], AX.X))(jc, wb[r]),
                 reads=[("wc", r)], writes=["scols"])

        ps_t = p.psum("ps_t", 32, 128, es=es)
        st32 = p.sb("st32", 32, 128, F32, es=es)
        s.op("tensor", lambda e: e.transpose(ps_t[:], scols[:], sident[:]),
             reads=["scols", "sident"], writes=["ps_t"])
        s.op("vector", lambda e: e.tensor_copy(st32[:], ps_t[:]),
             reads=["ps_t"], writes=["st32"])
        s.op("sync", lambda e: e.dma_start(cc_in, st32[:]), dma=True, reads=["st32"],
             writes=["cc_in"])
        s.op("gpsimd", lambda e: e.collective_compute(
            "AllGather", ALU.bypass, ins=[cc_in.opt()], outs=[cc_out.opt()],
            replica_groups=[list(range(NCORE))]),
            reads=["cc_in"], writes=["cc_out"], coll=True)
        ga = p.sb("ga", 128, 128, F32, es=es)
        gb = p.sb("gb", 128, 128, F32, es=es)
        gt = p.sb("gt", 128, 256, F32, es=es)
        ps_ga = p.psum("ps_ga", 128, 128, es=es)
        s.op("sync", lambda e: [
            e.dma_start(ga[:], cc_out[0:128, :]),
            e.dma_start(gb[:], cc_out[128:256, :])], dma=True, ndma=2,
            reads=["cc_out"], writes=["gab"])
        s.op("tensor", lambda e: e.transpose(ps_ga[:], ga[:], sident[:]),
             reads=["gab", "sident"], writes=["ps_ga"])
        s.op("vector", lambda e: e.tensor_copy(gt[:, 0:128], ps_ga[:]),
             reads=["ps_ga"], writes=["gt"])
        s.op("tensor", lambda e: e.transpose(ps_ga[:], gb[:], sident[:]),
             reads=["gab", "sident"], writes=["ps_ga"])
        s.op("vector", lambda e: e.tensor_copy(gt[:, 128:256], ps_ga[:]),
             reads=["ps_ga"], writes=["gt"])
        s.op("vector", lambda e: e.tensor_tensor(gt[:], gt[:], smask[:], ALU.mult),
             reads=["gt", "smask"], writes=["gt"])
        for jc in range(NCH):
            s.op("vector",
                 (lambda j: lambda e: e.reduce_sum(
                     spre[:, j:j + 1].unsqueeze(2),
                     bass.AP(gt.tensor if hasattr(gt, "tensor") else gt, j,
                             [[256, 128], [0, 1], [32, 8]]), AX.X))(jc),
                 reads=["gt"], writes=["spre"])
            s.op("vector",
                 (lambda j: lambda e: e.reduce_sum(
                     spre[:, 16 + j:16 + j + 1].unsqueeze(2),
                     bass.AP(gt.tensor if hasattr(gt, "tensor") else gt, 16 + j,
                             [[256, 128], [0, 1], [32, 8]]), AX.X))(jc),
                 reads=["gt"], writes=["spre"])

    with p.phase("y") as (s, es):
        kb = [p.sb(f"p4k{i}", 128, T, es=es) for i in range(2)]
        vb = [p.sb(f"p4v{i}", 128, T, es=es) for i in range(2)]
        qb = [p.sb(f"p4q{i}", 128, T, es=es) for i in range(2)]
        wb = [p.sb(f"p4w{i}", 128, T, es=es) for i in range(2)]
        cum1 = [p.sb(f"p4c{i}", 128, T, es=es) for i in range(2)]
        cum2 = [p.sb(f"p4d{i}", 128, T, es=es) for i in range(2)]
        for jc in range(NCH):
            r = jc % 2
            s.op("sync",
                 (lambda j, bb: lambda e: e.dma_start(bb[:], k_sp[j * 128:(j + 1) * 128, :]))(jc, kb[r]),
                 dma=True, writes=[("kc", r)])
            s.op("sync",
                 (lambda j, bb: lambda e: e.dma_start(bb[:], v_sp[j * 128:(j + 1) * 128, :]))(jc, vb[r]),
                 dma=True, writes=[("vc", r)])
            s.op("sync",
                 (lambda j, bb: lambda e: e.dma_start(bb[:], q_sp[j * 128:(j + 1) * 128, :]))(jc, qb[r]),
                 dma=True, writes=[("qc", r)])
            s.op("vector",
                 (lambda bb, ww: lambda e: e.tensor_tensor(ww[:], bb[:], cbRK[:], ALU.mult))(kb[r], wb[r]),
                 reads=[("kc", r), "cbRK"], writes=[("wc", r)])
            s.op("scalar",
                 (lambda ww: lambda e: e.activation(ww[:], ww[:], AF.Exp))(wb[r]),
                 reads=[("wc", r)], writes=[("wc", r)])
            s.op("vector",
                 (lambda j, ww, cc: lambda e: e.tensor_tensor_scan(
                     cc[:], onest[:], ww[:], spre[:, 16 + j:16 + j + 1],
                     ALU.mult, ALU.add))(jc, wb[r], cum2[r]),
                 reads=[("wc", r), "onest", "spre"], writes=[("cum2", r)])
            s.op("scalar",
                 (lambda cc: lambda e: e.activation(cc[:], cc[:], AF.Ln, bias=1e-6))(cum2[r]),
                 reads=[("cum2", r)], writes=[("cum2", r)])
            s.op("scalar",
                 (lambda cc: lambda e: e.activation(cc[:], cc[:], AF.Exp, scale=-1.0))(cum2[r]),
                 reads=[("cum2", r)], writes=[("cum2", r)])
            s.op("vector",
                 (lambda bb, ww: lambda e: e.tensor_tensor(ww[:], ww[:], bb[:], ALU.mult))(vb[r], wb[r]),
                 reads=[("wc", r), ("vc", r)], writes=[("wc", r)])
            s.op("vector",
                 (lambda j, ww, cc: lambda e: e.tensor_tensor_scan(
                     cc[:], onest[:], ww[:], spre[:, j:j + 1],
                     ALU.mult, ALU.add))(jc, wb[r], cum1[r]),
                 reads=[("wc", r), "onest", "spre"], writes=[("cum1", r)])
            s.op("vector",
                 (lambda bb: lambda e: e.tensor_tensor(bb[:], bb[:], cbRQ[:], ALU.mult))(qb[r]),
                 reads=[("qc", r), "cbRQ"], writes=[("qc", r)])
            s.op("scalar",
                 (lambda bb: lambda e: e.activation(bb[:], bb[:], AF.Sigmoid))(qb[r]),
                 reads=[("qc", r)], writes=[("qc", r)])
            s.op("vector",
                 (lambda bb, cc: lambda e: e.tensor_tensor(cc[:], cc[:], bb[:], ALU.mult))(qb[r], cum1[r]),
                 reads=[("qc", r), ("cum1", r)], writes=[("cum1", r)])
            s.op("vector",
                 (lambda j, cc, dd: lambda e: e.tensor_tensor(
                     A1[:, j * T:(j + 1) * T], cc[:], dd[:], ALU.mult))(jc, cum1[r], cum2[r]),
                 reads=[("cum1", r), ("cum2", r)], writes=[("A1", jc)])
        if dbg:
            s.op("sync", lambda e: [
                e.dma_start(dbg_y[k * 128:(k + 1) * 128, :], A1[:, k * T:(k + 1) * T])
                for k in range(NCH)], dma=True, ndma=NCH,
                deps=[s.last_writer[("A1", k)] for k in range(NCH)])
            s.op("sync", lambda e: [
                e.dma_start(dbg_cols[:, 0:32], scols[:]),
                e.dma_start(dbg_cols[:, 32:64], spre[:])], dma=True, ndma=2,
                reads=["scols", "spre"])

    with p.phase("sw") as (s, es):
        wq = [p.sb(f"wq{i}", 128, NCH * 256, es=es) for i in range(4)]
        trc = [p.sb(f"p5t{i}", 128, T, es=es) for i in range(2)]
        sgc = [p.sb(f"p5s{i}", 128, FT, es=es) for i in range(2)]
        psu = [p.psum(f"psu{i}", 128, FT, es=es) for i in range(2)]
        psg = [p.psum(f"psg{i}", 128, FT, es=es) for i in range(2)]
        NGRP = 32
        for g in range(NGRP):
            db = g % 2
            s.op("gpsimd",
                 (lambda gg, d: lambda e: [
                     e.dma_start(
                         wq[2 * d][:, k * 256:(k + 1) * 256],
                         wswT[k * 128:(k + 1) * 128, gg * 256:(gg + 1) * 256])
                     for k in range(NCH)])(g, db),
                 dma=True, ndma=NCH, writes=[("wu", db)])
            s.op("gpsimd",
                 (lambda gg, d: lambda e: [
                     e.dma_start(
                         wq[2 * d + 1][:, k * 256:(k + 1) * 256],
                         wswT[k * 128:(k + 1) * 128, 8192 + gg * 256:8192 + (gg + 1) * 256])
                     for k in range(NCH)])(g, db),
                 dma=True, ndma=NCH, writes=[("wg", db)])
            for fo2 in range(2):
                fo = g * 2 + fo2
                dst = trc[fo % 2]
                for tc in range(NTC):
                    pu, pg = psu[tc % 2], psg[tc % 2]
                    s.op("tensor",
                         (lambda d, f2, t, pp: lambda e: [
                             e.matmul(pp[:],
                                      wq[2 * d][:, k * 256 + f2 * 128: k * 256 + f2 * 128 + 128],
                                      A1[:, k * T + t * FT: k * T + t * FT + FT],
                                      start=(k == 0), stop=(k == NCH - 1))
                             for k in range(NCH)][-1])(db, fo2, tc, pu),
                         reads=[("wu", db)], writes=[("psu", tc % 2)])
                    s.op("tensor",
                         (lambda d, f2, t, pp: lambda e: [
                             e.matmul(pp[:],
                                      wq[2 * d + 1][:, k * 256 + f2 * 128: k * 256 + f2 * 128 + 128],
                                      A1[:, k * T + t * FT: k * T + t * FT + FT],
                                      start=(k == 0), stop=(k == NCH - 1))
                             for k in range(NCH)][-1])(db, fo2, tc, pg),
                         reads=[("wg", db)], writes=[("psg", tc % 2)])
                    s.op("scalar",
                         (lambda pp, ss: lambda e: e.activation(ss[:], pp[:], AF.Silu))(pg, sgc[tc % 2]),
                         reads=[("psg", tc % 2)], writes=[("sg", tc % 2)])
                    s.op("vector",
                         (lambda t, pp, ss, dd: lambda e: e.tensor_tensor(
                             dd[:, t * FT:(t + 1) * FT], pp[:], ss[:], ALU.mult))(tc, pu, sgc[tc % 2], dst),
                         reads=[("psu", tc % 2), ("sg", tc % 2)], writes=[("usg", fo % 2)])
                s.op("sync",
                     (lambda f, dd: lambda e: e.dma_start(
                         usg_sp[f * 128:(f + 1) * 128, :], dd[:]))(fo, dst),
                     dma=True, reads=[("usg", fo % 2)])

    es_a1.close()

    with p.phase("wc") as (s, es):
        wcb = [p.sb(f"wcb{i}", 128, T, es=es) for i in range(2)]
        for r in range(64):
            s.op("gpsimd",
                 (lambda rr, bb: lambda e: e.dma_start(
                     bb[:], woutT[rr * 128:(rr + 1) * 128, :]))(r, wcb[r % 2]),
                 dma=True, writes=[("wcb", r % 2)])
            s.op("sync",
                 (lambda rr, bb: lambda e: e.dma_start(
                     wout_bf[rr * 128:(rr + 1) * 128, :], bb[:]))(r, wcb[r % 2]),
                 dma=True, reads=[("wcb", r % 2)], writes=[("wbf", r // 8)])

    es_r3 = contextlib.ExitStack()
    cbC = p.sb("cbC", 128, T, F32, es=es_r3, side="right")
    cbD = p.sb("cbD", 128, T, F32, es=es_r3, side="right")
    with p.phase("cb2") as (s, es):
        r_m2 = p.sb("r_m2", 1, T, F32, es=es)
        r_r2 = p.sb("r_r2", 1, T, F32, es=es)
        psb = [p.psum(f"ps7b{i}", 128, FT, es=es) for i in range(2)]
        s.op("sync", lambda e: e.dma_start(r_m2[:], rows_sp[0:1, :]), dma=True,
             writes=["r_m2"])
        s.op("sync", lambda e: e.dma_start(r_r2[:], rows_sp[1:2, :]), dma=True,
             writes=["r_r2"])
        s.op("vector", lambda e: e.tensor_tensor(r_m2[:], r_m2[:], r_r2[:], ALU.mult),
             reads=["r_m2", "r_r2"], writes=["r_m2"])
        s.op("vector", lambda e: e.tensor_scalar(r_r2[:], r_r2[:], 1.0, None, ALU.add),
             reads=["r_r2"], writes=["r_r2"])
        for tc in range(NTC):
            s.op("tensor",
                 (lambda t: lambda e: e.matmul(
                     psb[0][:], onescol[:1, :], r_r2[:, t * FT:(t + 1) * FT],
                     start=True, stop=True))(tc),
                 reads=["r_r2", "onescol"], writes=[("psb", 0)])
            s.op("vector",
                 (lambda t: lambda e: e.tensor_copy(cbC[:, t * FT:(t + 1) * FT], psb[0][:]))(tc),
                 reads=[("psb", 0)], writes=["cbC"])
            s.op("tensor",
                 (lambda t: lambda e: e.matmul(
                     psb[1][:], onescol[:1, :], r_m2[:, t * FT:(t + 1) * FT],
                     start=True, stop=True))(tc),
                 reads=["r_m2", "onescol"], writes=[("psb", 1)])
            s.op("vector",
                 (lambda t: lambda e: e.tensor_copy(cbD[:, t * FT:(t + 1) * FT], psb[1][:]))(tc),
                 reads=[("psb", 1)], writes=["cbD"])

    with p.phase("out") as (s, es):
        ps8 = [p.psum(f"ps8_{i}", 128, FT, es=es) for i in range(8)]
        ub = [p.sb(f"ub{i}", 128, FT, es=es) for i in range(2)]
        wres = p.sb("wres", 128, 64 * 1024, es=es)
        xb = [p.sb(f"xb{i}", 128, FT, F32, es=es) for i in range(2)]
        ob = [p.sb(f"ob{i}", 128, FT, F32, es=es) for i in range(2)]
        out_dmas = []
        for dgrp in range(2):
            for fog in range(8):
                s.op("gpsimd",
                     (lambda fg, dg: lambda e: [
                         e.dma_start(
                             wres[:, (fg * 8 + j) * 1024:(fg * 8 + j) * 1024 + 1024],
                             bass.AP(wout_bf.tensor, ((fg * 8 + j) * 128) * CH + dg * 1024,
                                     [[CH, 128], [1, 1024]]))
                         for j in range(8)])(fog, dgrp),
                     dma=True, ndma=8, reads=[("wbf", fog)], writes=[("wres", fog)])
            for tc in range(NTC):
                for fo in range(64):
                    r = fo % 2
                    s.op("sync",
                         (lambda f, t, bb: lambda e: e.dma_start(
                             bb[:], bass.AP(usg_sp.tensor, (f * 128) * T + t * FT,
                                            [[T, 128], [1, FT]])))(fo, tc, ub[r]),
                         dma=True, writes=[("ub", r)])
                    for do in range(8):
                        s.op("tensor",
                             (lambda f, d, rr: lambda e: e.matmul(
                                 ps8[d][:],
                                 wres[:, f * 1024 + d * 128:f * 1024 + d * 128 + 128],
                                 ub[rr][:],
                                 start=(f == 0), stop=(f == 63)))(fo, do, r),
                             reads=[("ub", r), ("wres", fo // 8)], writes=[("ps8", do)])
                for do in range(8):
                    doch = dgrp * 8 + do
                    r = do % 2
                    s.op("sync",
                         (lambda dc, t, bb: lambda e: e.dma_start(
                             bb[:], x_fm[dc * 128:(dc + 1) * 128, t * FT:(t + 1) * FT]))(doch, tc, xb[r]),
                         dma=True, writes=[("xb", r)])
                    s.op("vector",
                         (lambda t, bb: lambda e: e.tensor_tensor(
                             bb[:], bb[:], cbC[:, t * FT:(t + 1) * FT], ALU.mult))(tc, xb[r]),
                         reads=[("xb", r), "cbC"], writes=[("xb", r)])
                    s.op("vector",
                         (lambda t, bb: lambda e: e.tensor_tensor(
                             bb[:], bb[:], cbD[:, t * FT:(t + 1) * FT], ALU.subtract))(tc, xb[r]),
                         reads=[("xb", r), "cbD"], writes=[("xb", r)])
                    s.op("vector",
                         (lambda d, bb, oo: lambda e: e.tensor_tensor(
                             oo[:], bb[:], ps8[d][:], ALU.add))(do, xb[r], ob[r]),
                         reads=[("xb", r), ("ps8", d) if False else ("ps8", do)], writes=[("ob", r)])
                    od = s.op("sync",
                              (lambda dc, t, oo: lambda e: e.dma_start(
                                  x1_fm[dc * 128:(dc + 1) * 128, t * FT:(t + 1) * FT],
                                  oo[:]))(doch, tc, ob[r]),
                              dma=True, reads=[("ob", r)])
                    out_dmas.append(od)
        s.op("gpsimd", lambda e: e.nop(), deps=out_dmas)
    es_r3.close()



NEXP = 16
TOPK = 1024
BISECT_ITERS = 30


def build_l0_ffn(env=None):
    if env is None:
        nc = bass.Bass("TRN2", target_bir_lowering=False)
        io = {}
        io["x1_fm"] = nc.dram_tensor("x1_fm", [CH, T], F32, kind="ExternalInput").ap()
        io["rw_in"] = nc.dram_tensor("rw_in", [CH, NEXP], F32, kind="ExternalInput").ap()
        io["rb_in"] = nc.dram_tensor("rb_in", [NEXP, 1], F32, kind="ExternalInput").ap()
        io["ew_in"] = nc.dram_tensor("ew_in", [CH, CH], F32, kind="ExternalInput").ap()
        io["ebc_in"] = nc.dram_tensor("ebc_in", [128, NCH], F32, kind="ExternalInput").ap()
        io["ones_in"] = nc.dram_tensor("ones_in", [1, 128], F32, kind="ExternalInput").ap()
        io["onesc_in"] = nc.dram_tensor("onesc_in", [128, 1], F32, kind="ExternalInput").ap()
        io["ident_in"] = nc.dram_tensor("ident_in", [128, 128], F32, kind="ExternalInput").ap()
        io["x2_fm"] = nc.dram_tensor("x2_fm", [CH, T], F32, kind="ExternalOutput").ap()
        io["l0f_cc_in"] = nc.dram_tensor("l0f_cc_in", [128, 256], F32, kind="Internal").ap()
        io["l0f_cc_out"] = nc.dram_tensor("l0f_cc_out", [NCORE * 128, 256], F32, kind="Internal").ap()
        io["rows_sp2"] = nc.dram_tensor("rows_sp2", [2, T], F32, kind="Internal").ap()
        io["gsr_sp"] = nc.dram_tensor("gsr_sp", [16, 128], F32, kind="Internal").ap()
        p = Prog(nc)
        ctx = _ctx_setup(p, nc, io)
    else:
        nc, p, io, ctx = env
    emit_l0_ffn(nc, p, io, ctx)
    if env is None:
        p.close()
        return nc
    return None


def emit_l0_ffn(nc, p, io, ctx):
    x1_fm, rw_in, rb_in, ew_in, ebc_in = (io["x1_fm"], io["rw_in"], io["rb_in"],
                                          io["ew_in"], io["ebc_in"])
    x2_fm = io["x2_fm"]
    cc_in, cc_out = io["l0f_cc_in"], io["l0f_cc_out"]
    rows_sp, gsr_sp = io["rows_sp2"], io["gsr_sp"]
    onescol, onescs, onescs_b, sident = (ctx["onescol"], ctx["onescs"],
                                         ctx["onescs_b"], ctx["sident"])
    srb = p.sb("srb", NEXP, 1, F32)
    sebc = p.sb("sebc", 128, NCH, F32)
    cbA = p.sb("cbA", 128, T)
    cbB = p.sb("cbB", 128, T)
    S_sb = p.sb("S_sb", 128, NCH * NEXP, F32)
    gsumb = p.sb("gsumb", 128, T, F32)
    A2 = p.sb("A2", 128, NCH * T)

    with p.phase("cst") as (s, es):
        s.op("sync", lambda e: e.dma_start(srb[:], rb_in), dma=True, writes=["srb"])
        s.op("sync", lambda e: e.dma_start(sebc[:], ebc_in), dma=True, writes=["sebc"])

    es_f1 = contextlib.ExitStack()
    r_mean = p.sb("r_mean", 1, T, F32, es=es_f1, side="right")
    r_var = p.sb("r_var", 1, T, F32, es=es_f1, side="right")
    r_r1 = p.sb("r_r1", 1, T, F32, es=es_f1, side="right")
    r_tmp = p.sb("r_tmp", 1, T, F32, es=es_f1, side="right")
    with p.phase("ln") as (s, es):
        trb = [p.sb(f"f1t{i}", 128, T, es=es) for i in range(3)]
        tfq = [p.sb(f"f1q{i}", 128, T, es=es) for i in range(2)]
        ps_sum = p.psum("ps_sum", 1, T, es=es)
        ps_sq = p.psum("ps_sq", 1, T, es=es)
        for k in range(NCH):
            b = trb[k % 3]
            s.op("gpsimd",
                 (lambda kk, bb: lambda e: e.dma_start(
                     bb[:], x1_fm[kk * 128:(kk + 1) * 128, :]))(k, b),
                 dma=True, writes=[("tr", k % 3)])
            s.op("tensor",
                 (lambda kk, bb: lambda e: [
                     e.matmul(ps_sum[:, tc * FT:(tc + 1) * FT], onescs_b[:, :],
                              bb[:, tc * FT:(tc + 1) * FT],
                              start=(kk == 0), stop=(kk == NCH - 1))
                     for tc in range(NTC)][-1])(k, b),
                 reads=[("tr", k % 3), "onescs_b"], writes=["ps_sum"])
            sq = tfq[k % 2]
            s.op("scalar",
                 (lambda bb, qq: lambda e: e.activation(qq[:], bb[:], AF.Square))(b, sq),
                 reads=[("tr", k % 3)], writes=[("sq", k % 2)])
            s.op("tensor",
                 (lambda kk, qq: lambda e: [
                     e.matmul(ps_sq[:, tc * FT:(tc + 1) * FT], onescs_b[:, :],
                              qq[:, tc * FT:(tc + 1) * FT],
                              start=(kk == 0), stop=(kk == NCH - 1))
                     for tc in range(NTC)][-1])(k, sq),
                 reads=[("sq", k % 2), "onescs_b"], writes=["ps_sq"])
        s.op("vector", lambda e: e.tensor_copy(r_mean[:], ps_sum[:1, :]),
             reads=["ps_sum"], writes=["r_mean"])
        s.op("vector", lambda e: e.tensor_copy(r_var[:], ps_sq[:1, :]),
             reads=["ps_sq"], writes=["r_var"])

    with p.phase("ln2") as (s, es):
        trb = [p.sb(f"f1b{i}", 128, T, es=es) for i in range(3)]
        s.op("vector", lambda e: e.tensor_scalar(r_mean[:], r_mean[:], 1.0 / CH, None, ALU.mult),
             reads=["r_mean"], writes=["r_mean"])
        s.op("vector", lambda e: e.tensor_scalar(r_var[:], r_var[:], 1.0 / CH, None, ALU.mult),
             reads=["r_var"], writes=["r_var"])
        s.op("vector", lambda e: e.tensor_tensor(r_tmp[:], r_mean[:], r_mean[:], ALU.mult),
             reads=["r_mean"], writes=["r_tmp"])
        s.op("vector", lambda e: e.tensor_tensor(r_var[:], r_var[:], r_tmp[:], ALU.subtract),
             reads=["r_tmp", "r_var"], writes=["r_var"])
        s.op("scalar", lambda e: e.activation(r_tmp[:], r_var[:], AF.Ln, bias=LN_EPS),
             reads=["r_var"], writes=["r_tmp"])
        s.op("scalar", lambda e: e.activation(r_r1[:], r_tmp[:], AF.Exp, scale=-0.5),
             reads=["r_tmp"], writes=["r_r1"])
        s.op("vector", lambda e: e.tensor_tensor(r_tmp[:], r_mean[:], r_r1[:], ALU.mult),
             reads=["r_mean", "r_r1"], writes=["r_tmp"])
        psb = [p.psum(f"psb{i}", 128, FT, es=es) for i in range(2)]
        for tc in range(NTC):
            s.op("tensor",
                 (lambda t: lambda e: e.matmul(
                     psb[0][:], onescol[:1, :], r_r1[:, t * FT:(t + 1) * FT],
                     start=True, stop=True))(tc),
                 reads=["r_r1", "onescol"], writes=[("psb", 0)])
            s.op("vector",
                 (lambda t: lambda e: e.tensor_copy(cbA[:, t * FT:(t + 1) * FT], psb[0][:]))(tc),
                 reads=[("psb", 0)], writes=["cbA"])
            s.op("tensor",
                 (lambda t: lambda e: e.matmul(
                     psb[1][:], onescol[:1, :], r_tmp[:, t * FT:(t + 1) * FT],
                     start=True, stop=True))(tc),
                 reads=["r_tmp", "onescol"], writes=[("psb", 1)])
            s.op("vector",
                 (lambda t: lambda e: e.tensor_copy(cbB[:, t * FT:(t + 1) * FT], psb[1][:]))(tc),
                 reads=[("psb", 1)], writes=["cbB"])
        for k in range(NCH):
            b = trb[k % 3]
            s.op("gpsimd",
                 (lambda kk, bb: lambda e: e.dma_start(
                     bb[:], x1_fm[kk * 128:(kk + 1) * 128, :]))(k, b),
                 dma=True, writes=[("tr", k % 3)])
            s.op("vector",
                 (lambda bb: lambda e: e.tensor_tensor(bb[:], bb[:], cbA[:], ALU.mult))(b),
                 reads=[("tr", k % 3), "cbA"], writes=[("tr", k % 3)])
            s.op("vector",
                 (lambda kk, bb: lambda e: e.tensor_tensor(
                     A2[:, kk * T:(kk + 1) * T], bb[:], cbB[:], ALU.subtract))(k, b),
                 reads=[("tr", k % 3), "cbB"], writes=[("A2", k)])

    es_f1.close()

    with p.phase("rt") as (s, es):
        srw = p.sb("srw", 128, NCH * NEXP, es=es)
        slog = p.sb("slog", NEXP, T, F32, es=es)
        ps_l = [p.psum(f"ps_l{i}", NEXP, FT, es=es) for i in range(2)]
        ps_tr = p.psum("ps_tr", 128, NEXP, es=es)
        st = [p.sb(f"f2s{i}", 128, NEXP, F32, es=es) for i in range(2)]
        rmx = p.sb("rmx", 128, 1, F32, es=es)
        rsm = p.sb("rsm", 128, 1, F32, es=es)
        s.op("gpsimd", lambda e: [
            e.dma_start(srw[:, k * NEXP:(k + 1) * NEXP], rw_in[k * 128:(k + 1) * 128, :])
            for k in range(NCH)], dma=True, ndma=NCH, writes=["srw"])
        for tc in range(NTC):
            s.op("tensor",
                 (lambda t: lambda e: [
                     e.matmul(ps_l[t % 2][:], srw[:, k * NEXP:(k + 1) * NEXP],
                              A2[:, k * T + t * FT:k * T + t * FT + FT],
                              start=(k == 0), stop=(k == NCH - 1))
                     for k in range(NCH)][-1])(tc),
                 reads=["srw"], writes=[("ps_l", tc % 2)])
            s.op("vector",
                 (lambda t: lambda e: e.tensor_scalar(
                     slog[:, t * FT:(t + 1) * FT], ps_l[t % 2][:], srb[:], None, ALU.add))(tc),
                 reads=[("ps_l", tc % 2), "srb"], writes=["slog"])
        for i in range(NCH):
            r = i % 2
            s.op("tensor",
                 (lambda ii: lambda e: e.transpose(
                     ps_tr[:], slog[:, ii * 128:(ii + 1) * 128], sident[0:NEXP, 0:NEXP]))(i),
                 reads=["slog", "sident"], writes=["ps_tr"])
            s.op("vector",
                 (lambda ss: lambda e: e.tensor_copy(ss[:], ps_tr[:]))(st[r]),
                 reads=["ps_tr"], writes=[("st", r)])
            s.op("vector",
                 (lambda ss: lambda e: e.reduce_max(rmx[:], ss[:], AX.X, negate=True))(st[r]),
                 reads=[("st", r)], writes=["rmx"])
            s.op("scalar",
                 (lambda ss: lambda e: e.activation(ss[:], ss[:], AF.Exp, bias=rmx[:]))(st[r]),
                 reads=[("st", r), "rmx"], writes=[("st", r)])
            s.op("vector",
                 (lambda ss: lambda e: e.reduce_sum(rsm[:], ss[:], AX.X))(st[r]),
                 reads=[("st", r)], writes=["rsm"])
            s.op("vector", lambda e: e.reciprocal(rsm[:], rsm[:]),
                 reads=["rsm"], writes=["rsm"])
            s.op("vector",
                 (lambda ii, ss: lambda e: e.tensor_scalar(
                     S_sb[:, ii * NEXP:(ii + 1) * NEXP], ss[:], rsm[:], None, ALU.mult))(i, st[r]),
                 reads=[("st", r), "rsm"], writes=["S_sb"])
        s.op("sync", lambda e: e.dma_start(cc_in, S_sb[:]), dma=True,
             reads=["S_sb"], writes=["cc_in"])
        s.op("gpsimd", lambda e: e.collective_compute(
            "AllGather", ALU.bypass, ins=[cc_in.opt()], outs=[cc_out.opt()],
            replica_groups=[list(range(NCORE))]),
            reads=["cc_in"], writes=["cc_out"], coll=True)

    with p.phase("bis") as (s, es):
        G = [p.sb(f"G{i}", 128, 256, F32, es=es) for i in range(NCORE)]
        taur = p.sb("taur", 1, 3 * NEXP, F32, es=es)
        taub = p.sb("taub", 128, 256, F32, es=es)
        cnt = p.sb("cnt", 128, NEXP, F32, es=es)
        cacc = p.sb("cacc", 128, NEXP, F32, es=es)
        crow = p.sb("crow", 1, NEXP, F32, es=es)
        mrow = p.sb("mrow", 1, NEXP, F32, es=es)
        mrow256 = p.sb("mrow256", 1, 256, F32, es=es)
        ge = p.sb("ge", 128, 256, F32, es=es)
        ps_c = p.psum("ps_c", 1, NEXP, es=es)
        ps_b = p.psum("ps_b", 128, 256, es=es)
        s.op("sync", lambda e: [
            e.dma_start(G[r][:], cc_out[r * 128:(r + 1) * 128, :])
            for r in range(NCORE)], dma=True, ndma=NCORE, writes=["G"])
        LO = lambda: taur[:, 0:NEXP]
        HI = lambda: taur[:, NEXP:2 * NEXP]
        MID = lambda: taur[:, 2 * NEXP:3 * NEXP]
        s.op("vector", lambda e: e.memset(LO(), 0.0), writes=["taulo"])
        s.op("vector", lambda e: e.memset(HI(), 1.0), writes=["tauhi"])
        for it in range(BISECT_ITERS):
            s.op("vector", lambda e: e.tensor_tensor(MID(), LO(), HI(), ALU.add),
                 reads=["taulo", "tauhi"], writes=["taumid"])
            s.op("vector", lambda e: e.tensor_scalar(MID(), MID(), 0.5, None, ALU.mult),
                 reads=["taumid"], writes=["taumid"])
            s.op("vector", lambda e: e.tensor_copy(
                mrow256[:].rearrange("p (a b) -> p a b", a=NCH),
                bass.AP(taur, 2 * NEXP, [[3 * NEXP, 1], [0, NCH], [1, NEXP]])),
                reads=["taumid"], writes=["mrow256"])
            s.op("tensor", lambda e: e.matmul(
                ps_b[:], onescol[:1, :], mrow256[:], start=True, stop=True),
                reads=["mrow256", "onescol"], writes=["ps_b"])
            s.op("vector", lambda e: e.tensor_copy(taub[:], ps_b[:]),
                 reads=["ps_b"], writes=["taub"])
            for r in range(NCORE):
                s.op("vector",
                     (lambda rr: lambda e: e.tensor_tensor(ge[:], G[rr][:], taub[:], ALU.is_ge))(r),
                     reads=["G", "taub"], writes=["ge"])
                s.op("vector",
                     lambda e: e.reduce_sum(
                         cnt[:].unsqueeze(2),
                         bass.AP(ge, 0, [[256, 128], [1, NEXP], [NEXP, NCH]]), AX.X),
                     reads=["ge"], writes=["cnt"])
                if r == 0:
                    s.op("vector", lambda e: e.tensor_copy(cacc[:], cnt[:]),
                         reads=["cnt"], writes=["cacc"])
                else:
                    s.op("vector", lambda e: e.tensor_tensor(cacc[:], cacc[:], cnt[:], ALU.add),
                         reads=["cnt", "cacc"], writes=["cacc"])
            s.op("tensor", lambda e: e.matmul(ps_c[:], onescs[:, :], cacc[:],
                                              start=True, stop=True),
                 reads=["cacc", "onescs"], writes=["ps_c"])
            s.op("vector", lambda e: e.tensor_copy(crow[:], ps_c[:1, :]),
                 reads=["ps_c"], writes=["crow"])
            s.op("vector", lambda e: e.tensor_scalar(crow[:], crow[:], float(TOPK), None, ALU.is_ge),
                 reads=["crow"], writes=["crow"])
            s.op("vector", lambda e: e.tensor_tensor(mrow[:], MID(), LO(), ALU.subtract),
                 reads=["taumid", "taulo"], writes=["mrow"])
            s.op("vector", lambda e: e.tensor_tensor(mrow[:], mrow[:], crow[:], ALU.mult),
                 reads=["mrow", "crow"], writes=["mrow"])
            s.op("vector", lambda e: e.tensor_tensor(LO(), LO(), mrow[:], ALU.add),
                 reads=["mrow", "taulo"], writes=["taulo"])
            s.op("vector", lambda e: e.tensor_tensor(mrow[:], HI(), MID(), ALU.subtract),
                 reads=["tauhi", "taumid"], writes=["mrow"])
            s.op("vector", lambda e: e.tensor_tensor(mrow[:], mrow[:], crow[:], ALU.mult),
                 reads=["mrow", "crow"], writes=["mrow"])
            s.op("vector", lambda e: e.tensor_tensor(HI(), MID(), mrow[:], ALU.add),
                 reads=["taumid", "mrow"], writes=["tauhi"])
        s.op("vector", lambda e: e.tensor_copy(
            mrow256[:].rearrange("p (a b) -> p a b", a=NCH),
            bass.AP(taur, 0, [[3 * NEXP, 1], [0, NCH], [1, NEXP]])),
            reads=["taulo"], writes=["mrow256"])
        s.op("tensor", lambda e: e.matmul(
            ps_b[:], onescol[:1, :], mrow256[:], start=True, stop=True),
            reads=["mrow256", "onescol"], writes=["ps_b"])
        s.op("vector", lambda e: e.tensor_copy(taub[:], ps_b[:]),
             reads=["ps_b"], writes=["taub"])
        s.op("vector", lambda e: e.tensor_tensor(ge[:], S_sb[:], taub[:], ALU.is_ge),
             reads=["S_sb", "taub"], writes=["ge"])
        s.op("vector", lambda e: e.tensor_tensor(ge[:], ge[:], S_sb[:], ALU.mult),
             reads=["ge", "S_sb"], writes=["ge"])
        gsum16 = p.sb("gsum16", 128, NCH, F32, es=es)
        s.op("vector", lambda e: e.reduce_sum(
            gsum16[:].unsqueeze(2),
            bass.AP(ge, 0, [[256, 128], [NEXP, NCH], [1, NEXP]]), AX.X),
            reads=["ge"], writes=["gsum16"])
        ps_g = p.psum("ps_g", NEXP, 128, es=es)
        g16 = p.sb("g16r", NEXP, 128, F32, es=es)
        grow = p.sb("grow", 1, T, F32, es=es)
        s.op("tensor", lambda e: e.transpose(ps_g[:], gsum16[:], sident[:]),
             reads=["gsum16", "sident"], writes=["ps_g"])
        s.op("vector", lambda e: e.tensor_copy(g16[:], ps_g[:]),
             reads=["ps_g"], writes=["g16r"])
        s.op("sync", lambda e: e.dma_start(gsr_sp, g16[:]), dma=True,
             reads=["g16r"], writes=["gsr_sp"])
        s.op("sync", lambda e: e.dma_start(grow[:], bass.AP(gsr_sp.tensor, 0, [[T, 1], [1, T]])),
             dma=True, reads=["gsr_sp"], writes=["grow"])
        psb2 = [p.psum(f"psb2{i}", 128, FT, es=es) for i in range(2)]
        for tc in range(NTC):
            s.op("tensor",
                 (lambda t: lambda e: e.matmul(
                     psb2[t % 2][:], onescol[:1, :], grow[:, t * FT:(t + 1) * FT],
                     start=True, stop=True))(tc),
                 reads=["grow", "onescol"], writes=[("psb2", tc % 2)])
            s.op("vector",
                 (lambda t: lambda e: e.tensor_copy(gsumb[:, t * FT:(t + 1) * FT], psb2[t % 2][:]))(tc),
                 reads=[("psb2", tc % 2)], writes=["gsumb"])

    with p.phase("z") as (s, es):
        web = p.sb("web", 128, 2 * NCH * FT, es=es)
        xb = [p.sb(f"f4x{i}", 128, FT, F32, es=es) for i in range(2)]
        ob = [p.sb(f"f4o{i}", 128, FT, F32, es=es) for i in range(2)]
        psz = [p.psum(f"psz{i}", 128, FT, es=es) for i in range(4)]
        out_dmas = []
        NG = NCH // 4
        for grp in range(NG):
            half = grp % 2
            s.op("gpsimd",
                 (lambda g, h: lambda e: [
                     e.dma_start(
                         web[:, h * NCH * FT + k * FT:h * NCH * FT + (k + 1) * FT],
                         ew_in[k * 128:(k + 1) * 128, g * FT:(g + 1) * FT])
                     for k in range(NCH)])(grp, half),
                 dma=True, ndma=NCH, writes=[("web", half)])
            for mo in range(4):
                do = grp * 4 + mo
                for tc in range(NTC):
                    i = do * NTC + tc
                    pt = psz[i % 4]
                    s.op("tensor",
                         (lambda h, m, t, pp: lambda e: [
                             e.matmul(pp[:],
                                      web[:, h * NCH * FT + k * FT + m * 128:
                                           h * NCH * FT + k * FT + m * 128 + 128],
                                      A2[:, k * T + t * FT:k * T + t * FT + FT],
                                      start=(k == 0), stop=(k == NCH - 1))
                             for k in range(NCH)][-1])(half, mo, tc, pt),
                         reads=[("web", half)], writes=[("psz", i % 4)])
                    r = i % 2
                    s.op("sync",
                         (lambda d, t, bb: lambda e: e.dma_start(
                             bb[:], x1_fm[d * 128:(d + 1) * 128, t * FT:(t + 1) * FT]))(do, tc, xb[r]),
                         dma=True, writes=[("xb", r)])
                    s.op("vector",
                         (lambda d, t, pp, oo: lambda e: e.scalar_tensor_tensor(
                             oo[:], pp[:], sebc[:, d:d + 1], gsumb[:, t * FT:(t + 1) * FT],
                             ALU.add, ALU.mult))(do, tc, pt, ob[r]),
                         reads=[("psz", i % 4), "sebc", "gsumb"], writes=[("ob", r)])
                    s.op("vector",
                         (lambda bb, oo: lambda e: e.tensor_tensor(oo[:], oo[:], bb[:], ALU.add))(xb[r], ob[r]),
                         reads=[("xb", r), ("ob", r)], writes=[("ob", r)])
                    od = s.op("sync",
                              (lambda d, t, oo: lambda e: e.dma_start(
                                  x2_fm[d * 128:(d + 1) * 128, t * FT:(t + 1) * FT], oo[:]))(do, tc, ob[r]),
                              dma=True, reads=[("ob", r)])
                    out_dmas.append(od)
        s.op("gpsimd", lambda e: e.nop(), deps=out_dmas)


def _l0_ffn_inmaps(x1, router_w, router_b, expert_w, expert_b):
    ones = np.ones((1, 128), dtype=np.float32)
    onesc = np.ones((128, 1), dtype=np.float32)
    ident = np.eye(128, dtype=np.float32)
    rb = np.ascontiguousarray(router_b.reshape(NEXP, 1))
    ebc = np.ascontiguousarray(expert_b.reshape(NCH, 128).T)
    ims = []
    for c in range(NCORE):
        b, h = _core_meta(c)
        xs = np.ascontiguousarray(x1[b, h * T:(h + 1) * T, :].T)
        ims.append(dict(x1_fm=xs, rw_in=router_w, rb_in=rb, ew_in=expert_w,
                        ebc_in=ebc, ones_in=ones, onesc_in=onesc, ident_in=ident))
    return ims


def run_l0_ffn(x1, router_w, router_b, expert_w, expert_b):
    nc = build_l0_ffn()
    ims = _l0_ffn_inmaps(x1, router_w, router_b, expert_w, expert_b)
    res = run_bass_kernel_spmd(nc, ims, core_ids=list(range(NCORE)))
    x2 = np.empty((4, 4096, CH), dtype=np.float32)
    for c in range(NCORE):
        b, h = _core_meta(c)
        x2[b, h * T:(h + 1) * T, :] = res.results[c]["x2_fm"].T
    return x2



DI = 4096
NDI = DI // 128


def build_l1_pre(env=None):
    if env is None:
        nc = bass.Bass("TRN2", target_bir_lowering=False)
        io = {}
        io["x2_fm"] = nc.dram_tensor("x2_fm", [CH, T], F32, kind="ExternalInput").ap()
        io["ipT"] = nc.dram_tensor("ipT", [CH, 2 * DI], F32, kind="ExternalInput").ap()
        io["xpT"] = nc.dram_tensor("xpT", [DI, 160], F32, kind="ExternalInput").ap()
        io["dpT"] = nc.dram_tensor("dpT", [128, DI], F32, kind="ExternalInput").ap()
        io["cwc_in"] = nc.dram_tensor("cwc_in", [128, NDI * 4], F32, kind="ExternalInput").ap()
        io["cbc_in"] = nc.dram_tensor("cbc_in", [128, NDI], F32, kind="ExternalInput").ap()
        io["dbc_in"] = nc.dram_tensor("dbc_in", [128, NDI], F32, kind="ExternalInput").ap()
        io["xprev_in"] = nc.dram_tensor("xprev_in", [CH, 4], F32, kind="ExternalInput").ap()
        io["ones_in"] = nc.dram_tensor("ones_in", [1, 128], F32, kind="ExternalInput").ap()
        io["onesc_in"] = nc.dram_tensor("onesc_in", [128, 1], F32, kind="ExternalInput").ap()
        io["ident_in"] = nc.dram_tensor("ident_in", [128, 128], F32, kind="ExternalInput").ap()
        io["u_out"] = nc.dram_tensor("u_out", [DI, T], BF16, kind="ExternalOutput").ap()
        io["dl_out"] = nc.dram_tensor("dl_out", [DI, T], F32, kind="ExternalOutput").ap()
        io["sz_out"] = nc.dram_tensor("sz_out", [DI, T], BF16, kind="ExternalOutput").ap()
        io["B_out"] = nc.dram_tensor("B_out", [16, T], F32, kind="ExternalOutput").ap()
        io["C_out"] = nc.dram_tensor("C_out", [16, T], F32, kind="ExternalOutput").ap()
        io["xc_sp"] = nc.dram_tensor("xc_sp", [DI, T], BF16, kind="Internal").ap()
        io["up_sp"] = nc.dram_tensor("up_sp", [DI, T], BF16, kind="Internal").ap()
        p = Prog(nc)
        ctx = _ctx_setup(p, nc, io)
    else:
        nc, p, io, ctx = env
    emit_l1_pre(nc, p, io, ctx)
    if env is None:
        p.close()
        return nc
    return None


def emit_l1_pre(nc, p, io, ctx, xprev_sb=None):
    x2_fm, ipT, xpT, dpT = io["x2_fm"], io["ipT"], io["xpT"], io["dpT"]
    cwc_in, cbc_in, dbc_in = io["cwc_in"], io["cbc_in"], io["dbc_in"]
    xprev_in = io.get("xprev_in")
    u_out, dl_out, sz_out = io["u_out"], io["dl_out"], io["sz_out"]
    B_out, C_out = io["B_out"], io["C_out"]
    xc_sp, up_sp = io["xc_sp"], io["up_sp"]
    onescol, onescs_b = ctx["onescol"], ctx["onescs_b"]
    scw = p.sb("scw", 128, NDI * 4, F32)
    scb = p.sb("scb", 128, NDI, F32)
    sdb = p.sb("sdb", 128, NDI, F32)
    sxprev = p.sb("sxprev", 128, NCH * 4)
    cbA = p.sb("cbA", 128, T)
    cbB = p.sb("cbB", 128, T)
    sdtl = p.sb("sdtl", 128, T)
    A2 = p.sb("A2", 128, NCH * T)

    with p.phase("cst") as (s, es):
        s.op("sync", lambda e: e.dma_start(scw[:], cwc_in), dma=True, writes=["scw"])
        s.op("sync", lambda e: e.dma_start(scb[:], cbc_in), dma=True, writes=["scb"])
        s.op("sync", lambda e: e.dma_start(sdb[:], dbc_in), dma=True, writes=["sdb"])
        if xprev_sb is None:
            s.op("gpsimd", lambda e: [
                e.dma_start(sxprev[:, k * 4:(k + 1) * 4], xprev_in[k * 128:(k + 1) * 128, :])
                for k in range(NCH)], dma=True, ndma=NCH, writes=["sxprev"])
        else:
            src_sp, zmask = xprev_sb
            s.op("gpsimd", lambda e: [
                e.dma_start(sxprev[:, k * 4:(k + 1) * 4], src_sp[k * 128:(k + 1) * 128, :])
                for k in range(NCH)], dma=True, ndma=NCH, writes=["sxprev"])
            s.op("vector", lambda e: e.tensor_scalar(sxprev[:], sxprev[:], zmask[:], None, ALU.mult),
                 reads=["sxprev", "zmask"], writes=["sxprev"])

    es_f1 = contextlib.ExitStack()
    r_mean = p.sb("r_mean", 1, T, F32, es=es_f1, side="right")
    r_var = p.sb("r_var", 1, T, F32, es=es_f1, side="right")
    r_r1 = p.sb("r_r1", 1, T, F32, es=es_f1, side="right")
    r_tmp = p.sb("r_tmp", 1, T, F32, es=es_f1, side="right")
    with p.phase("ln") as (s, es):
        trb = [p.sb(f"g1t{i}", 128, T, es=es) for i in range(3)]
        tfq = [p.sb(f"g1q{i}", 128, T, es=es) for i in range(2)]
        ps_sum = p.psum("ps_sum", 1, T, es=es)
        ps_sq = p.psum("ps_sq", 1, T, es=es)
        for k in range(NCH):
            b = trb[k % 3]
            s.op("gpsimd",
                 (lambda kk, bb: lambda e: e.dma_start(
                     bb[:], x2_fm[kk * 128:(kk + 1) * 128, :]))(k, b),
                 dma=True, writes=[("tr", k % 3)])
            s.op("tensor",
                 (lambda kk, bb: lambda e: [
                     e.matmul(ps_sum[:, tc * FT:(tc + 1) * FT], onescs_b[:, :],
                              bb[:, tc * FT:(tc + 1) * FT],
                              start=(kk == 0), stop=(kk == NCH - 1))
                     for tc in range(NTC)][-1])(k, b),
                 reads=[("tr", k % 3), "onescs_b"], writes=["ps_sum"])
            sq = tfq[k % 2]
            s.op("scalar",
                 (lambda bb, qq: lambda e: e.activation(qq[:], bb[:], AF.Square))(b, sq),
                 reads=[("tr", k % 3)], writes=[("sq", k % 2)])
            s.op("tensor",
                 (lambda kk, qq: lambda e: [
                     e.matmul(ps_sq[:, tc * FT:(tc + 1) * FT], onescs_b[:, :],
                              qq[:, tc * FT:(tc + 1) * FT],
                              start=(kk == 0), stop=(kk == NCH - 1))
                     for tc in range(NTC)][-1])(k, sq),
                 reads=[("sq", k % 2), "onescs_b"], writes=["ps_sq"])
        s.op("vector", lambda e: e.tensor_copy(r_mean[:], ps_sum[:1, :]),
             reads=["ps_sum"], writes=["r_mean"])
        s.op("vector", lambda e: e.tensor_copy(r_var[:], ps_sq[:1, :]),
             reads=["ps_sq"], writes=["r_var"])

    with p.phase("ln2") as (s, es):
        trb = [p.sb(f"g2t{i}", 128, T, es=es) for i in range(3)]
        psb = [p.psum(f"psb{i}", 128, FT, es=es) for i in range(2)]
        s.op("vector", lambda e: e.tensor_scalar(r_mean[:], r_mean[:], 1.0 / CH, None, ALU.mult),
             reads=["r_mean"], writes=["r_mean"])
        s.op("vector", lambda e: e.tensor_scalar(r_var[:], r_var[:], 1.0 / CH, None, ALU.mult),
             reads=["r_var"], writes=["r_var"])
        s.op("vector", lambda e: e.tensor_tensor(r_tmp[:], r_mean[:], r_mean[:], ALU.mult),
             reads=["r_mean"], writes=["r_tmp"])
        s.op("vector", lambda e: e.tensor_tensor(r_var[:], r_var[:], r_tmp[:], ALU.subtract),
             reads=["r_tmp", "r_var"], writes=["r_var"])
        s.op("scalar", lambda e: e.activation(r_tmp[:], r_var[:], AF.Ln, bias=LN_EPS),
             reads=["r_var"], writes=["r_tmp"])
        s.op("scalar", lambda e: e.activation(r_r1[:], r_tmp[:], AF.Exp, scale=-0.5),
             reads=["r_tmp"], writes=["r_r1"])
        s.op("vector", lambda e: e.tensor_tensor(r_tmp[:], r_mean[:], r_r1[:], ALU.mult),
             reads=["r_mean", "r_r1"], writes=["r_tmp"])
        for tc in range(NTC):
            s.op("tensor",
                 (lambda t: lambda e: e.matmul(
                     psb[0][:], onescol[:1, :], r_r1[:, t * FT:(t + 1) * FT],
                     start=True, stop=True))(tc),
                 reads=["r_r1", "onescol"], writes=[("psb", 0)])
            s.op("vector",
                 (lambda t: lambda e: e.tensor_copy(cbA[:, t * FT:(t + 1) * FT], psb[0][:]))(tc),
                 reads=[("psb", 0)], writes=["cbA"])
            s.op("tensor",
                 (lambda t: lambda e: e.matmul(
                     psb[1][:], onescol[:1, :], r_tmp[:, t * FT:(t + 1) * FT],
                     start=True, stop=True))(tc),
                 reads=["r_tmp", "onescol"], writes=[("psb", 1)])
            s.op("vector",
                 (lambda t: lambda e: e.tensor_copy(cbB[:, t * FT:(t + 1) * FT], psb[1][:]))(tc),
                 reads=[("psb", 1)], writes=["cbB"])
        for k in range(NCH):
            b = trb[k % 3]
            s.op("gpsimd",
                 (lambda kk, bb: lambda e: e.dma_start(
                     bb[:], x2_fm[kk * 128:(kk + 1) * 128, :]))(k, b),
                 dma=True, writes=[("tr", k % 3)])
            s.op("vector",
                 (lambda bb: lambda e: e.tensor_tensor(bb[:], bb[:], cbA[:], ALU.mult))(b),
                 reads=[("tr", k % 3), "cbA"], writes=[("tr", k % 3)])
            s.op("vector",
                 (lambda kk, bb: lambda e: e.tensor_tensor(
                     A2[:, kk * T:(kk + 1) * T], bb[:], cbB[:], ALU.subtract))(k, b),
                 reads=[("tr", k % 3), "cbB"], writes=[("A2", k)])
        pxs = p.psum("pxs", 1, 4, es=es)
        pxq = p.psum("pxq", 1, 4, es=es)
        rx = p.sb("rx", 1, 16, F32, es=es)
        sqx = p.sb("sqx", 128, NCH * 4, es=es)
        s.op("scalar", lambda e: e.activation(sqx[:], sxprev[:], AF.Square),
             reads=["sxprev"], writes=["sqx"])
        s.op("tensor", lambda e: [
            e.matmul(pxs[:], onescs_b[:, :],
                     bass.AP(sxprev, k * 4, [[NCH * 4, 128], [1, 4]]),
                     start=(k == 0), stop=(k == NCH - 1))
            for k in range(NCH)][-1],
            reads=["sxprev", "onescs_b"], writes=["pxs"])
        s.op("tensor", lambda e: [
            e.matmul(pxq[:], onescs_b[:, :],
                     bass.AP(sqx, k * 4, [[NCH * 4, 128], [1, 4]]),
                     start=(k == 0), stop=(k == NCH - 1))
            for k in range(NCH)][-1],
            reads=["sqx", "onescs_b"], writes=["pxq"])
        s.op("vector", lambda e: e.tensor_scalar(rx[:, 0:4], pxs[:1, :], 1.0 / CH, None, ALU.mult),
             reads=["pxs"], writes=["rx"])
        s.op("vector", lambda e: e.tensor_scalar(rx[:, 4:8], pxq[:1, :], 1.0 / CH, None, ALU.mult),
             reads=["pxq"], writes=["rx"])
        s.op("vector", lambda e: e.tensor_tensor(rx[:, 8:12], rx[:, 0:4], rx[:, 0:4], ALU.mult),
             reads=["rx"], writes=["rx"])
        s.op("vector", lambda e: e.tensor_tensor(rx[:, 4:8], rx[:, 4:8], rx[:, 8:12], ALU.subtract),
             reads=["rx"], writes=["rx"])
        s.op("scalar", lambda e: e.activation(rx[:, 4:8], rx[:, 4:8], AF.Ln, bias=LN_EPS),
             reads=["rx"], writes=["rx"])
        s.op("scalar", lambda e: e.activation(rx[:, 4:8], rx[:, 4:8], AF.Exp, scale=-0.5),
             reads=["rx"], writes=["rx"])
        s.op("vector", lambda e: e.tensor_tensor(rx[:, 8:12], rx[:, 0:4], rx[:, 4:8], ALU.mult),
             reads=["rx"], writes=["rx"])
        pxb = p.psum("pxb", 128, 8, es=es)
        s.op("tensor", lambda e: e.matmul(pxb[:, 0:4], onescol[:1, :], rx[:, 4:8],
                                          start=True, stop=True),
             reads=["rx", "onescol"], writes=["pxb"])
        s.op("tensor", lambda e: e.matmul(pxb[:, 4:8], onescol[:1, :], rx[:, 8:12],
                                          start=True, stop=True),
             reads=["rx", "onescol"], writes=["pxb"])
        s.op("vector", lambda e: e.tensor_copy(sqx[:, 0:8], pxb[:]),
             reads=["pxb"], writes=["sqx"])
        for k in range(NCH):
            s.op("vector",
                 (lambda kk: lambda e: e.tensor_tensor(
                     sxprev[:, kk * 4:(kk + 1) * 4], sxprev[:, kk * 4:(kk + 1) * 4],
                     sqx[:, 0:4], ALU.mult))(k),
                 reads=["sxprev", "sqx"], writes=["sxprev"])
            s.op("vector",
                 (lambda kk: lambda e: e.tensor_tensor(
                     sxprev[:, kk * 4:(kk + 1) * 4], sxprev[:, kk * 4:(kk + 1) * 4],
                     sqx[:, 4:8], ALU.subtract))(k),
                 reads=["sxprev", "sqx"], writes=["sxprev"])
    es_f1.close()

    es_xp = contextlib.ExitStack()
    sxpc = p.sb("sxpc", 128, NDI * 4, es=es_xp, side="right")
    with p.phase("ip") as (s, es):
        wbuf = p.sb("wbuf", 128, 2 * NCH * FT, es=es)
        trb = [p.sb(f"g3t{i}", 128, T, es=es) for i in range(4)]
        pxp = p.psum("pxp", 128, 4, es=es)
        psq = [p.psum(f"psq{i}", 128, FT, es=es) for i in range(4)]
        NMO = 2 * DI // 128
        NG = NMO // 4
        for grp in range(NG):
            half = grp % 2
            s.op("gpsimd",
                 (lambda g, h: lambda e: [
                     e.dma_start(
                         wbuf[:, h * NCH * FT + k * FT:h * NCH * FT + (k + 1) * FT],
                         ipT[k * 128:(k + 1) * 128, g * FT:(g + 1) * FT])
                     for k in range(NCH)])(grp, half),
                 dma=True, ndma=NCH, writes=[("wbuf", half)])
            for mo in range(4):
                mo_g = grp * 4 + mo
                dst = trb[mo_g % 4]
                for tc in range(NTC):
                    i = mo_g * NTC + tc
                    pt = psq[i % 4]
                    s.op("tensor",
                         (lambda h, m, t, pp: lambda e: [
                             e.matmul(pp[:],
                                      wbuf[:, h * NCH * FT + k * FT + m * 128:
                                           h * NCH * FT + k * FT + m * 128 + 128],
                                      A2[:, k * T + t * FT:k * T + t * FT + FT],
                                      start=(k == 0), stop=(k == NCH - 1))
                             for k in range(NCH)][-1])(half, mo, tc, pt),
                         reads=[("wbuf", half)], writes=[("psq", i % 4)])
                    if mo_g < NDI:
                        s.op("vector",
                             (lambda dd, t, pp: lambda e: e.tensor_copy(
                                 dd[:, t * FT:(t + 1) * FT], pp[:]))(dst, tc, pt),
                             reads=[("psq", i % 4)], writes=[("chunk", mo_g % 4)])
                    else:
                        s.op("scalar",
                             (lambda dd, t, pp: lambda e: e.activation(
                                 dd[:, t * FT:(t + 1) * FT], pp[:], AF.Silu))(dst, tc, pt),
                             reads=[("psq", i % 4)], writes=[("chunk", mo_g % 4)])
                if mo_g < NDI:
                    s.op("tensor",
                         (lambda h, m: lambda e: [
                             e.matmul(pxp[:],
                                      wbuf[:, h * NCH * FT + k * FT + m * 128:
                                           h * NCH * FT + k * FT + m * 128 + 128],
                                      sxprev[:, k * 4:(k + 1) * 4],
                                      start=(k == 0), stop=(k == NCH - 1))
                             for k in range(NCH)][-1])(half, mo),
                         reads=[("wbuf", half), "sxprev"], writes=["pxp"])
                    s.op("vector",
                         (lambda mg: lambda e: e.tensor_copy(
                             sxpc[:, mg * 4:(mg + 1) * 4], pxp[:]))(mo_g),
                         reads=["pxp"], writes=["sxpc"])
                    s.op("sync",
                         (lambda mg, dd: lambda e: e.dma_start(
                             xc_sp[mg * 128:(mg + 1) * 128, :], dd[:]))(mo_g, dst),
                         dma=True, reads=[("chunk", mo_g % 4)])
                else:
                    s.op("sync",
                         (lambda mg, dd: lambda e: e.dma_start(
                             sz_out[(mg - NDI) * 128:(mg - NDI) * 128 + 128, :], dd[:]))(mo_g, dst),
                         dma=True, reads=[("chunk", mo_g % 4)])

    with p.phase("cv") as (s, es):
        wcat = [p.sb(f"cv{i}", 128, T + 4, es=es) for i in range(2)]
        acc = [p.sb(f"cva{i}", 128, T, F32, es=es) for i in range(2)]
        ub = [p.sb(f"cvu{i}", 128, T, es=es) for i in range(2)]
        for mo in range(NDI):
            r = mo % 2
            s.op("sync",
                 (lambda m, ww: lambda e: e.dma_start(
                     ww[:, 4:4 + T], xc_sp[m * 128:(m + 1) * 128, :]))(mo, wcat[r]),
                 dma=True, writes=[("wc", r)])
            s.op("vector",
                 (lambda m, ww: lambda e: e.tensor_copy(
                     ww[:, 1:4], sxpc[:, m * 4 + 1:m * 4 + 4]))(mo, wcat[r]),
                 reads=["sxpc"], writes=[("wc", r)])
            s.op("vector",
                 (lambda m, ww, aa: lambda e: e.tensor_scalar(
                     aa[:], ww[:, 1:1 + T], scw[:, m * 4:m * 4 + 1], None, ALU.mult))(mo, wcat[r], acc[r]),
                 reads=[("wc", r), "scw"], writes=[("cacc", r)])
            for kk in range(1, 4):
                s.op("vector",
                     (lambda m, k2, ww, aa: lambda e: e.scalar_tensor_tensor(
                         aa[:], ww[:, 1 + k2:1 + k2 + T], scw[:, m * 4 + k2:m * 4 + k2 + 1],
                         aa[:], ALU.mult, ALU.add))(mo, kk, wcat[r], acc[r]),
                     reads=[("wc", r), "scw", ("cacc", r)], writes=[("cacc", r)])
            s.op("scalar",
                 (lambda m, aa, uu: lambda e: e.activation(
                     uu[:], aa[:], AF.Silu, bias=scb[:, m:m + 1]))(mo, acc[r], ub[r]),
                 reads=[("cacc", r), "scb"], writes=[("ub", r)])
            s.op("sync",
                 (lambda m, uu: lambda e: e.dma_start(
                     u_out[m * 128:(m + 1) * 128, :], uu[:]))(mo, ub[r]),
                 dma=True, reads=[("ub", r)])
            s.op("sync",
                 (lambda m, uu: lambda e: e.dma_start(
                     up_sp[m * 128:(m + 1) * 128, :], uu[:]))(mo, ub[r]),
                 dma=True, reads=[("ub", r)])
    es_xp.close()

    with p.phase("xd") as (s, es):
        sxp = p.sb("sxp", 128, NDI * 160, es=es)
        ubb = [p.sb(f"xdu{i}", 128, T, es=es) for i in range(2)]
        sbc = p.sb("sbc", 32, T, F32, es=es)
        ps_dtl = [p.psum(f"pdtl{i}", 128, FT, es=es) for i in range(NTC)]
        ps_bc = [p.psum(f"pbc{i}", 32, FT, es=es) for i in range(NTC)]
        s.op("gpsimd", lambda e: [
            e.dma_start(sxp[:, k * 160:(k + 1) * 160], xpT[k * 128:(k + 1) * 128, :])
            for k in range(NDI)], dma=True, ndma=NDI, writes=["sxp"])
        for k in range(NDI):
            r = k % 2
            s.op("sync",
                 (lambda kk, uu: lambda e: e.dma_start(
                     uu[:], up_sp[kk * 128:(kk + 1) * 128, :]))(k, ubb[r]),
                 dma=True, writes=[("ux", r)])
            for tc in range(NTC):
                s.op("tensor",
                     (lambda kk, t, uu: lambda e: e.matmul(
                         ps_dtl[t][:], sxp[:, kk * 160:kk * 160 + 128],
                         uu[:, t * FT:(t + 1) * FT],
                         start=(kk == 0), stop=(kk == NDI - 1)))(k, tc, ubb[r]),
                     reads=[("ux", r), "sxp"], writes=[("pdtl", tc)])
                s.op("tensor",
                     (lambda kk, t, uu: lambda e: e.matmul(
                         ps_bc[t][:], sxp[:, kk * 160 + 128:kk * 160 + 160],
                         uu[:, t * FT:(t + 1) * FT],
                         start=(kk == 0), stop=(kk == NDI - 1)))(k, tc, ubb[r]),
                     reads=[("ux", r), "sxp"], writes=[("pbc", tc)])
        for tc in range(NTC):
            s.op("vector",
                 (lambda t: lambda e: e.tensor_copy(
                     sdtl[:, t * FT:(t + 1) * FT], ps_dtl[t][:]))(tc),
                 reads=[("pdtl", tc)], writes=["sdtl"])
            s.op("vector",
                 (lambda t: lambda e: e.tensor_copy(
                     sbc[:, t * FT:(t + 1) * FT], ps_bc[t][:]))(tc),
                 reads=[("pbc", tc)], writes=["sbc"])
        s.op("sync", lambda e: e.dma_start(B_out, sbc[0:16, :]), dma=True, reads=["sbc"])
        s.op("sync", lambda e: e.dma_start(C_out, sbc[16:32, :]), dma=True, reads=["sbc"])

    with p.phase("dt") as (s, es):
        sdp = p.sb("sdp", 128, DI, es=es)
        ta = [p.sb(f"dta{i}", 128, FT, F32, es=es) for i in range(2)]
        tb = [p.sb(f"dtb{i}", 128, FT, F32, es=es) for i in range(2)]
        td = [p.sb(f"dtd{i}", 128, FT, F32, es=es) for i in range(2)]
        psq = [p.psum(f"pdt{i}", 128, FT, es=es) for i in range(4)]
        s.op("gpsimd", lambda e: [
            e.dma_start(sdp[:, m * 128:(m + 1) * 128], dpT[:, m * 128:(m + 1) * 128])
            for m in range(NDI)], dma=True, ndma=NDI, writes=["sdp"])
        for mo in range(NDI):
            for tc in range(NTC):
                i = mo * NTC + tc
                pt = psq[i % 4]
                r = i % 2
                s.op("tensor",
                     (lambda m, t, pp: lambda e: e.matmul(
                         pp[:], sdp[:, m * 128:(m + 1) * 128],
                         sdtl[:, t * FT:(t + 1) * FT],
                         start=True, stop=True))(mo, tc, pt),
                     reads=["sdp", "sdtl"], writes=[("pdt", i % 4)])
                s.op("scalar",
                     (lambda m, pp, aa: lambda e: e.activation(
                         aa[:], pp[:], AF.Abs, bias=sdb[:, m:m + 1]))(mo, pt, ta[r]),
                     reads=[("pdt", i % 4), "sdb"], writes=[("ta", r)])
                s.op("scalar",
                     (lambda aa: lambda e: e.activation(aa[:], aa[:], AF.Exp, scale=-1.0))(ta[r]),
                     reads=[("ta", r)], writes=[("ta", r)])
                s.op("scalar",
                     (lambda aa: lambda e: e.activation(aa[:], aa[:], AF.Ln, bias=1.0))(ta[r]),
                     reads=[("ta", r)], writes=[("ta", r)])
                s.op("scalar",
                     (lambda m, pp, bb: lambda e: e.activation(
                         bb[:], pp[:], AF.Relu, bias=sdb[:, m:m + 1]))(mo, pt, tb[r]),
                     reads=[("pdt", i % 4), "sdb"], writes=[("tb", r)])
                s.op("vector",
                     (lambda aa, bb, dd: lambda e: e.tensor_tensor(
                         dd[:], aa[:], bb[:], ALU.add))(ta[r], tb[r], td[r]),
                     reads=[("ta", r), ("tb", r)], writes=[("td", r)])
                od = s.op("sync",
                          (lambda m, t, dd: lambda e: e.dma_start(
                              dl_out[m * 128:(m + 1) * 128, t * FT:(t + 1) * FT], dd[:]))(mo, tc, td[r]),
                          dma=True, reads=[("td", r)])
        s.op("gpsimd", lambda e: e.nop(), deps=[od])



def _core_meta(c):
    return c // 2, c % 2


def _l0_attn_inmaps(x, w_qkv, w_swiglu, w_out):
    wqkvT = np.ascontiguousarray(w_qkv.T)
    wswT = np.ascontiguousarray(w_swiglu.T)
    woutT = np.ascontiguousarray(w_out.T)
    ones = np.ones((1, 128), dtype=np.float32)
    onesc = np.ones((128, 1), dtype=np.float32)
    ident = np.eye(128, dtype=np.float32)
    ims = []
    for c in range(NCORE):
        b, h = _core_meta(c)
        xs = np.ascontiguousarray(x[b, h * T:(h + 1) * T, :].T)
        mask = np.zeros(8, dtype=np.float32)
        if h == 1:
            mask[2 * b] = 1.0
        smask = np.tile(np.repeat(mask, 32), (128, 1)).astype(np.float32)
        ims.append(dict(x_fm=xs, wqkvT=wqkvT, wswT=wswT, woutT=woutT,
                        ones_in=ones, onesc_in=onesc, ident_in=ident, smask_in=smask))
    return ims


def run_l0_attn(x, w_qkv, w_swiglu, w_out):
    nc = build_l0_attn()
    ims = _l0_attn_inmaps(x, w_qkv, w_swiglu, w_out)
    res = run_bass_kernel_spmd(nc, ims, core_ids=list(range(NCORE)))
    x1 = np.empty((4, 4096, CH), dtype=np.float32)
    for c in range(NCORE):
        b, h = _core_meta(c)
        x1[b, h * T:(h + 1) * T, :] = res.results[c]["x1_fm"].T
    return x1



LFULL = 4096
LC = 1024
NLC = LFULL // LC
NCK = 16
NST = 16


def build_l1_scan(dbg=False, env=None):
    if env is None:
        nc = bass.Bass("TRN2", target_bir_lowering=False)
        io = {}
        io["dl_in"] = nc.dram_tensor("dl_in", [CH, LFULL], F32, kind="ExternalInput").ap()
        io["u_in"] = nc.dram_tensor("u_in", [CH, LFULL], BF16, kind="ExternalInput").ap()
        io["sz_in"] = nc.dram_tensor("sz_in", [CH, LFULL], BF16, kind="ExternalInput").ap()
        io["B_in"] = nc.dram_tensor("B_in", [NST, LFULL], F32, kind="ExternalInput").ap()
        io["C_in"] = nc.dram_tensor("C_in", [NST, LFULL], F32, kind="ExternalInput").ap()
        io["alog_in"] = nc.dram_tensor("alog_in", [128, NCK * NST], F32, kind="ExternalInput").ap()
        io["d_in"] = nc.dram_tensor("d_in", [128, NCK], F32, kind="ExternalInput").ap()
        io["ones_in"] = nc.dram_tensor("ones_in", [1, 128], F32, kind="ExternalInput").ap()
        io["onesc_in"] = nc.dram_tensor("onesc_in", [128, 1], F32, kind="ExternalInput").ap()
        io["ident_in"] = nc.dram_tensor("ident_in", [128, 128], F32, kind="ExternalInput").ap()
        io["yg_out"] = nc.dram_tensor("yg_out", [CH, LFULL], BF16, kind="ExternalOutput").ap()
        p = Prog(nc)
        ctx = _ctx_setup(p, nc, io)
    else:
        nc, p, io, ctx = env
    emit_l1_scan(nc, p, io, ctx)
    if env is None:
        p.close()
        return nc
    return None


def emit_l1_scan(nc, p, io, ctx):
    dbg = False
    dl_in, u_in, sz_in = io["dl_in"], io["u_in"], io["sz_in"]
    B_in, C_in, alog_in, d_in = io["B_in"], io["C_in"], io["alog_in"], io["d_in"]
    yg_out = io["yg_out"]
    onescol = ctx["onescol"]
    sA = p.sb("sA", 128, NCK * NST, F32)
    sD = p.sb("sD", 128, NCK, F32)
    carry = p.sb("carry", 128, NCK * NST, F32)
    Bx = p.sb("Bx", 128, NST * LC)
    Cx = p.sb("Cx", 128, NST * LC)

    with p.phase("cst") as (s, es):
        s.op("sync", lambda e: e.dma_start(sA[:], alog_in), dma=True, writes=["sA"])
        s.op("sync", lambda e: e.dma_start(sD[:], d_in), dma=True, writes=["sD"])
        s.op("vector", lambda e: e.memset(carry[:], 0.0), writes=["carry"])
        s.op("scalar", lambda e: e.activation(sA[:], sA[:], AF.Exp),
             reads=["sA"], writes=["sA"])
        s.op("vector", lambda e: e.tensor_scalar(sA[:], sA[:], -1.0, None, ALU.mult),
             reads=["sA"], writes=["sA"])

    with p.phase("scan") as (s, es):
        ps_bc = p.psum("ps_bc", 128, LC, es=es)
        rowst = p.sb("rowst", 1, LC, F32, es=es)
        dlb = [p.sb(f"dl{i}", 128, LC, F32, es=es) for i in range(2)]
        ub = [p.sb(f"u{i}", 128, LC, es=es) for i in range(2)]
        szb = [p.sb(f"sz{i}", 128, LC, es=es) for i in range(2)]
        dub = [p.sb(f"du{i}", 128, LC, es=es) for i in range(2)]
        ab = [p.sb(f"a{i}", 128, LC, es=es) for i in range(2)]
        wb = [p.sb(f"w{i}", 128, LC, es=es) for i in range(2)]
        hb = [p.sb(f"h{i}", 128, LC, es=es) for i in range(2)]
        hcb = [p.sb(f"hc{i}", 128, LC, es=es) for i in range(2)]
        ya = [p.sb(f"ya{i}", 128, LC, F32, es=es) for i in range(2)]
        ygb = [p.sb(f"yg{i}", 128, LC, es=es) for i in range(2)]
        for lc in range(NLC):
            for n in range(NST):
                for src_d, dst_t, dkey in ((B_in, Bx, "Bx"), (C_in, Cx, "Cx")):
                    s.op("sync",
                         (lambda nn, sd, l2: lambda e: e.dma_start(
                             rowst[:], sd[nn:nn + 1, l2 * LC:(l2 + 1) * LC]))(n, src_d, lc),
                         dma=True, writes=["rowst"])
                    for half in range(2):
                        s.op("tensor",
                             (lambda hh2: lambda e: e.matmul(
                                 ps_bc[:, hh2 * FT:(hh2 + 1) * FT], onescol[:1, :],
                                 rowst[:, hh2 * FT:(hh2 + 1) * FT],
                                 start=True, stop=True))(half),
                             reads=["rowst", "onescol"], writes=[("ps_bc", half)])
                    for half in range(2):
                        s.op("scalar",
                             (lambda nn, dt2, hh3: lambda e: e.activation(
                                 dt2[:, nn * LC + hh3 * FT:nn * LC + (hh3 + 1) * FT],
                                 ps_bc[:, hh3 * FT:(hh3 + 1) * FT], AF.Copy))(n, dst_t, half),
                             reads=[("ps_bc", half)], writes=[dkey])
            for chk in range(NCK):
                r = chk % 2
                s.op("sync",
                     (lambda c2, bb, l2: lambda e: e.dma_start(
                         bb[:], dl_in[c2 * 128:(c2 + 1) * 128, l2 * LC:(l2 + 1) * LC]))(chk, dlb[r], lc),
                     dma=True, writes=[("dl", r)])
                s.op("sync",
                     (lambda c2, bb, l2: lambda e: e.dma_start(
                         bb[:], u_in[c2 * 128:(c2 + 1) * 128, l2 * LC:(l2 + 1) * LC]))(chk, ub[r], lc),
                     dma=True, writes=[("u", r)])
                s.op("sync",
                     (lambda c2, bb, l2: lambda e: e.dma_start(
                         bb[:], sz_in[c2 * 128:(c2 + 1) * 128, l2 * LC:(l2 + 1) * LC]))(chk, szb[r], lc),
                     dma=True, writes=[("sz", r)])
                s.op("vector",
                     (lambda bb, uu, dd: lambda e: e.tensor_tensor(
                         dd[:], bb[:], uu[:], ALU.mult))(dlb[r], ub[r], dub[r]),
                     reads=[("dl", r), ("u", r)], writes=[("du", r)])
                for n in range(NST):
                    rn = n % 2
                    col = chk * NST + n
                    s.op("scalar",
                         (lambda c3, bb, aa: lambda e: e.activation(
                             aa[:], bb[:], AF.Exp, scale=sA[:, c3:c3 + 1]))(col, dlb[r], ab[rn]),
                         reads=[("dl", r), "sA"], writes=[("a", rn)])
                    s.op("vector",
                         (lambda nn, dd, ww: lambda e: e.tensor_tensor(
                             ww[:], dd[:], Bx[:, nn * LC:(nn + 1) * LC], ALU.mult))(n, dub[r], wb[rn]),
                         reads=[("du", r), "Bx"], writes=[("w", rn)])
                    s.op("vector",
                         (lambda c3, aa, ww, hh2: lambda e: e.tensor_tensor_scan(
                             hh2[:], aa[:], ww[:], carry[:, c3:c3 + 1],
                             ALU.mult, ALU.add))(col, ab[rn], wb[rn], hb[rn]),
                         reads=[("a", rn), ("w", rn), ("carry", col)], writes=[("h", rn)])
                    s.op("scalar",
                         (lambda c3, hh2: lambda e: e.activation(
                             carry[:, c3:c3 + 1], hh2[:, LC - 1:LC], AF.Copy))(col, hb[rn]),
                         reads=[("h", rn)], writes=[("carry", col)])
                    if n == 0:
                        s.op("vector",
                             (lambda nn, hh2, yy: lambda e: e.tensor_tensor(
                                 yy[:], hh2[:], Cx[:, nn * LC:(nn + 1) * LC], ALU.mult))(n, hb[rn], ya[r]),
                             reads=[("h", rn), "Cx"], writes=[("ya", r)])
                    else:
                        s.op("vector",
                             (lambda nn, hh2, cc: lambda e: e.tensor_tensor(
                                 cc[:], hh2[:], Cx[:, nn * LC:(nn + 1) * LC], ALU.mult))(n, hb[rn], hcb[rn]),
                             reads=[("h", rn), "Cx"], writes=[("hc", rn)])
                    if dbg and lc == 0 and chk == 0 and n == 0:
                        s.op("sync", (lambda aa, ww, hh3: lambda e: [
                            e.dma_start(dbg_a, aa[:]),
                            e.dma_start(dbg_w, ww[:]),
                            e.dma_start(dbg_h, hh3[:]),
                            e.dma_start(dbg_bx, Bx[:, 0:LC])])(ab[rn], wb[rn], hb[rn]),
                            dma=True, ndma=4,
                            reads=[("a", rn), ("w", rn), ("h", rn), "Bx"])
                    if n > 0:
                        s.op("vector",
                             (lambda cc, yy: lambda e: e.tensor_tensor(
                                 yy[:], yy[:], cc[:], ALU.add))(hcb[rn], ya[r]),
                             reads=[("hc", rn), ("ya", r)], writes=[("ya", r)])
                if dbg and lc == 0 and chk == 0:
                    s.op("sync", (lambda yy: lambda e: e.dma_start(dbg_ya, yy[:]))(ya[r]),
                         dma=True, reads=[("ya", r)])
                s.op("vector",
                     (lambda c2, uu, yy: lambda e: e.scalar_tensor_tensor(
                         yy[:], uu[:], sD[:, c2:c2 + 1], yy[:],
                         ALU.mult, ALU.add))(chk, ub[r], ya[r]),
                     reads=[("u", r), "sD", ("ya", r)], writes=[("ya", r)])
                s.op("vector",
                     (lambda yy, zz, gg: lambda e: e.tensor_tensor(
                         gg[:], yy[:], zz[:], ALU.mult))(ya[r], szb[r], ygb[r]),
                     reads=[("ya", r), ("sz", r)], writes=[("yg", r)])
                od = s.op("sync",
                          (lambda c2, gg, l2: lambda e: e.dma_start(
                              yg_out[c2 * 128:(c2 + 1) * 128, l2 * LC:(l2 + 1) * LC], gg[:]))(chk, ygb[r], lc),
                          dma=True, reads=[("yg", r)])
        s.op("gpsimd", lambda e: e.nop(), deps=[od])


def _l1_scan_inmaps(pre_results, ins):
    alog = ins['L1_attn_A_log']
    Dv = ins['L1_attn_D']
    ones = np.ones((1, 128), dtype=np.float32)
    ims = []
    for c in range(NCORE):
        b, hh = c // 2, c % 2
        ch0 = hh * CH
        r0 = pre_results[2 * b]
        r1 = pre_results[2 * b + 1]
        dl = np.concatenate([r0["dl_out"][ch0:ch0 + CH], r1["dl_out"][ch0:ch0 + CH]], axis=1)
        uu = np.concatenate([r0["u_out"][ch0:ch0 + CH], r1["u_out"][ch0:ch0 + CH]], axis=1)
        sz = np.concatenate([r0["sz_out"][ch0:ch0 + CH], r1["sz_out"][ch0:ch0 + CH]], axis=1)
        Bm = np.concatenate([r0["B_out"], r1["B_out"]], axis=1)
        Cm = np.concatenate([r0["C_out"], r1["C_out"]], axis=1)
        asl = alog[ch0:ch0 + CH].reshape(NCK, 128, NST)
        asl = np.ascontiguousarray(np.transpose(asl, (1, 0, 2)).reshape(128, NCK * NST))
        dsl = np.ascontiguousarray(Dv[ch0:ch0 + CH].reshape(NCK, 128).T)
        ims.append(dict(dl_in=np.ascontiguousarray(dl), u_in=np.ascontiguousarray(uu),
                        sz_in=np.ascontiguousarray(sz), B_in=np.ascontiguousarray(Bm),
                        C_in=np.ascontiguousarray(Cm), alog_in=asl, d_in=dsl, ones_in=ones))
    return ims


def run_l1_scan(pre_results, ins):
    nc = build_l1_scan()
    ims = _l1_scan_inmaps(pre_results, ins)
    res = run_bass_kernel_spmd(nc, ims, core_ids=list(range(NCORE)))
    return res.results



def build_l1_post(env=None):
    if env is None:
        nc = bass.Bass("TRN2", target_bir_lowering=False)
        io = {}
        io["x2_fm"] = nc.dram_tensor("x2_fm", [CH, T], F32, kind="ExternalInput").ap()
        io["yg_fm"] = nc.dram_tensor("yg_fm", [DI, T], BF16, kind="ExternalInput").ap()
        io["opT"] = nc.dram_tensor("opT", [DI, CH], F32, kind="ExternalInput").ap()
        io["x3_fm"] = nc.dram_tensor("x3_fm", [CH, T], F32, kind="ExternalOutput").ap()
        io["op_bf"] = nc.dram_tensor("op_bf", [DI, CH], BF16, kind="Internal").ap()
        p = Prog(nc)
        ctx = None
    else:
        nc, p, io, ctx = env
    emit_l1_post(nc, p, io, ctx)
    if env is None:
        p.close()
        return nc
    return None


def emit_l1_post(nc, p, io, ctx):
    x2_fm, yg_fm, opT = io["x2_fm"], io["yg_fm"], io["opT"]
    x3_fm, op_bf = io["x3_fm"], io["op_bf"]
    with p.phase("wc") as (s, es):
        wcb = [p.sb(f"wcb{i}", 128, CH, es=es) for i in range(2)]
        for r in range(NDI):
            s.op("gpsimd",
                 (lambda rr, bb: lambda e: e.dma_start(
                     bb[:], opT[rr * 128:(rr + 1) * 128, :]))(r, wcb[r % 2]),
                 dma=True, writes=[("wcb", r % 2)])
            s.op("sync",
                 (lambda rr, bb: lambda e: e.dma_start(
                     op_bf[rr * 128:(rr + 1) * 128, :], bb[:]))(r, wcb[r % 2]),
                 dma=True, reads=[("wcb", r % 2)])
    with p.phase("op") as (s, es):
        ygt = p.sb("ygt", 128, NDI * FT, es=es)
        wb = [p.sb(f"opw{i}", 128, 1024, es=es) for i in range(2)]
        xb = [p.sb(f"opx{i}", 128, FT, F32, es=es) for i in range(2)]
        ob = [p.sb(f"opo{i}", 128, FT, F32, es=es) for i in range(2)]
        ps8 = [p.psum(f"op8_{i}", 128, FT, es=es) for i in range(8)]
        out_dmas = []
        for tc in range(NTC):
            s.op("sync",
                 (lambda t: lambda e: [
                     e.dma_start(ygt[:, k * FT:(k + 1) * FT],
                                 yg_fm[k * 128:(k + 1) * 128, t * FT:(t + 1) * FT])
                     for k in range(NDI)])(tc),
                 dma=True, ndma=NDI, writes=["ygt"])
            for dgrp in range(2):
                for k in range(NDI):
                    r = k % 2
                    s.op("sync",
                         (lambda kk, dg, bb: lambda e: e.dma_start(
                             bb[:], bass.AP(op_bf.tensor, (kk * 128) * CH + dg * 1024,
                                            [[CH, 128], [1, 1024]])))(k, dgrp, wb[r]),
                         dma=True, writes=[("opw", r)])
                    for do in range(8):
                        s.op("tensor",
                             (lambda kk, d, rr: lambda e: e.matmul(
                                 ps8[d][:], wb[rr][:, d * 128:(d + 1) * 128],
                                 ygt[:, kk * FT:(kk + 1) * FT],
                                 start=(kk == 0), stop=(kk == NDI - 1)))(k, do, r),
                             reads=[("opw", r), "ygt"], writes=[("op8", do)])
                for do in range(8):
                    doch = dgrp * 8 + do
                    r = do % 2
                    s.op("sync",
                         (lambda dc, t, bb: lambda e: e.dma_start(
                             bb[:], x2_fm[dc * 128:(dc + 1) * 128, t * FT:(t + 1) * FT]))(doch, tc, xb[r]),
                         dma=True, writes=[("opx", r)])
                    s.op("vector",
                         (lambda d, bb, oo: lambda e: e.tensor_tensor(
                             oo[:], bb[:], ps8[d][:], ALU.add))(do, xb[r], ob[r]),
                         reads=[("opx", r), ("op8", do)], writes=[("opo", r)])
                    od = s.op("sync",
                              (lambda dc, t, oo: lambda e: e.dma_start(
                                  x3_fm[dc * 128:(dc + 1) * 128, t * FT:(t + 1) * FT], oo[:]))(doch, tc, ob[r]),
                              dma=True, reads=[("opo", r)])
                    out_dmas.append(od)
        s.op("gpsimd", lambda e: e.nop(), deps=out_dmas)


def run_l1_post(x2, yg_full, ins):
    opT = np.ascontiguousarray(ins['L1_attn_out_proj_w'].T)
    ims = []
    for c in range(NCORE):
        b, h = _core_meta(c)
        xs = np.ascontiguousarray(x2[b, h * T:(h + 1) * T, :].T)
        yg = np.ascontiguousarray(yg_full[b][:, h * T:(h + 1) * T])
        ims.append(dict(x2_fm=xs, yg_fm=yg, opT=opT))
    nc = build_l1_post()
    res = run_bass_kernel_spmd(nc, ims, core_ids=list(range(NCORE)))
    x3 = np.empty((4, 4096, CH), dtype=np.float32)
    for c in range(NCORE):
        b, h = _core_meta(c)
        x3[b, h * T:(h + 1) * T, :] = res.results[c]["x3_fm"].T
    return x3



INTER = 5504
NIC = INTER // 128


def build_l1_mat(env=None):
    if env is None:
        nc = bass.Bass("TRN2", target_bir_lowering=False)
        io = {}
        io["gidx"] = nc.dram_tensor("gidx", [256, INTER], I32, kind="ExternalInput").ap()
        io["uidx"] = nc.dram_tensor("uidx", [256, INTER], I32, kind="ExternalInput").ap()
        io["didx"] = nc.dram_tensor("didx", [768, CH], I32, kind="ExternalInput").ap()
        io["lutc_in"] = nc.dram_tensor("lutc_in", [128, 16], F32, kind="ExternalInput").ap()
        io["wg_out"] = nc.dram_tensor("wg_out", [256, INTER], BF16, kind="ExternalOutput").ap()
        io["wu_out"] = nc.dram_tensor("wu_out", [256, INTER], BF16, kind="ExternalOutput").ap()
        io["wd_out"] = nc.dram_tensor("wd_out", [768, CH], BF16, kind="ExternalOutput").ap()
        p = Prog(nc)
        ctx = None
    else:
        nc, p, io, ctx = env
    emit_l1_mat(nc, p, io, ctx)
    if env is None:
        p.close()
        return nc
    return None


def emit_l1_mat(nc, p, io, ctx):
    gidx, uidx, didx, lutc_in = io["gidx"], io["uidx"], io["didx"], io["lutc_in"]
    wg_out, wu_out, wd_out = io.get("wg_out"), io.get("wu_out"), io.get("wd_out")
    parts = {"wg": io.get("wg_out_parts"), "wu": io.get("wu_out_parts"),
             "wd": io.get("wd_out_parts")}
    chunks = {"wg": io.get("wg_chunk_cols"), "wu": io.get("wu_chunk_cols"),
              "wd": io.get("wd_chunk_cols")}
    lutc = p.sb("lutc", 128, 16, F32)
    with p.phase("cst") as (s, es):
        s.op("sync", lambda e: e.dma_start(lutc[:], lutc_in), dma=True, writes=["lutc"])
    with p.phase("mat") as (s, es):
        F = INTER
        sidx = [p.sb(f"mi{i}", 128, F, dt=I32, es=es) for i in range(2)]
        sf = [p.sb(f"mf{i}", 128, F, es=es) for i in range(2)]
        accv = [p.sb(f"mav{i}", 128, F, es=es) for i in range(2)]
        ctb = [p.sb(f"mc{i}", 128, F, es=es) for i in range(2)]
        tiles = ([(gidx, "wg", wg_out, i, INTER) for i in range(2)] +
                 [(uidx, "wu", wu_out, i, INTER) for i in range(2)] +
                 [(didx, "wd", wd_out, i, CH) for i in range(6)])
        for ti, (src, wkey, dst, row, width) in enumerate(tiles):
            r = ti % 2
            s.op("sync",
                 (lambda ss, rr, wd2, bb: lambda e: e.dma_start(
                     bb[:, 0:wd2], ss[rr * 128:(rr + 1) * 128, :]))(src, row, width, sidx[r]),
                 dma=True, writes=[("si", r)])
            s.op("vector",
                 (lambda wd2, bb, ff: lambda e: e.tensor_copy(ff[:, 0:wd2], bb[:, 0:wd2]))(width, sidx[r], sf[r]),
                 reads=[("si", r)], writes=[("sf", r)])
            for j in range(16):
                acc, ct, akey, ckey = accv[r], ctb[r], ("av", r), ("ct", r)
                jj = j
                s.op("vector",
                     (lambda wd2, ff, cc, j2: lambda e: e.tensor_scalar(
                         cc[:, 0:wd2], ff[:, 0:wd2], float(j2), lutc[:, j2:j2 + 1],
                         ALU.is_equal, ALU.mult))(width, sf[r], ct, jj),
                     reads=[("sf", r), "lutc"], writes=[ckey])
                if j == 0:
                    s.op("vector",
                         (lambda wd2, cc, aa: lambda e: e.tensor_copy(aa[:, 0:wd2], cc[:, 0:wd2]))(width, ct, acc),
                         reads=[ckey], writes=[akey])
                else:
                    s.op("vector",
                         (lambda wd2, cc, aa: lambda e: e.tensor_tensor(
                             aa[:, 0:wd2], aa[:, 0:wd2], cc[:, 0:wd2], ALU.add))(width, ct, acc),
                         reads=[ckey, akey], writes=[akey])
            if parts[wkey] is None:
                od = s.op("sync",
                          (lambda dd, rr, wd2, oo: lambda e: e.dma_start(
                              dd[rr * 128:(rr + 1) * 128, :], oo[:, 0:wd2]))(dst, row, width, accv[r]),
                          dma=True, reads=[("av", r)])
            else:
                plist = parts[wkey]
                cws = chunks[wkey]
                def spill(e, rr=row, oo=accv[r], plist=plist, cws=cws):
                    outs = []
                    off2 = 0
                    for pt, w2 in zip(plist, cws):
                        outs.append(e.dma_start(pt[rr * 128:(rr + 1) * 128, :],
                                                oo[:, off2:off2 + w2]))
                        off2 += w2
                    return outs
                od = s.op("sync", spill, dma=True, ndma=len(plist), reads=[("av", r)])
        s.op("gpsimd", lambda e: e.nop(), deps=[od])


def run_l1_mat(ins):
    gidxT = np.ascontiguousarray(ins['L1_ffn_gate_idx'].T)
    uidxT = np.ascontiguousarray(ins['L1_ffn_up_idx'].T)
    didxT = np.ascontiguousarray(ins['L1_ffn_down_idx'].T)
    lutc = np.ascontiguousarray(np.tile(ins['L1_ffn_lut'], (128, 1))).astype(np.float32)
    ims = []
    for c in range(NCORE):
        gs = gidxT[c * 256:(c + 1) * 256]
        us = uidxT[c * 256:(c + 1) * 256]
        dtiles = np.zeros((768, CH), dtype=np.int32)
        for i in range(6):
            t0 = (c + 8 * i)
            if t0 < NIC:
                dtiles[i * 128:(i + 1) * 128] = didxT[t0 * 128:(t0 + 1) * 128]
        ims.append(dict(gidx=np.ascontiguousarray(gs), uidx=np.ascontiguousarray(us),
                        didx=dtiles, lutc_in=lutc))
    nc = build_l1_mat()
    res = run_bass_kernel_spmd(nc, ims, core_ids=list(range(NCORE)))
    import ml_dtypes
    wgT = np.empty((CH, INTER), dtype=ml_dtypes.bfloat16)
    wuT = np.empty((CH, INTER), dtype=ml_dtypes.bfloat16)
    wdT = np.empty((INTER, CH), dtype=ml_dtypes.bfloat16)
    for c in range(NCORE):
        wgT[c * 256:(c + 1) * 256] = res.results[c]["wg_out"]
        wuT[c * 256:(c + 1) * 256] = res.results[c]["wu_out"]
        for i in range(6):
            t0 = c + 8 * i
            if t0 < NIC:
                wdT[t0 * 128:(t0 + 1) * 128] = res.results[c]["wd_out"][i * 128:(i + 1) * 128]
    return wgT, wuT, wdT



def build_l1_ffn(env=None):
    if env is None:
        nc = bass.Bass("TRN2", target_bir_lowering=False)
        io = {}
        io["x3_fm"] = nc.dram_tensor("x3_fm", [CH, T], F32, kind="ExternalInput").ap()
        io["wgT_in"] = nc.dram_tensor("wgT_in", [CH, INTER], BF16, kind="ExternalInput").ap()
        io["wuT_in"] = nc.dram_tensor("wuT_in", [CH, INTER], BF16, kind="ExternalInput").ap()
        io["wdT_in"] = nc.dram_tensor("wdT_in", [INTER, CH], BF16, kind="ExternalInput").ap()
        io["ones_in"] = nc.dram_tensor("ones_in", [1, 128], F32, kind="ExternalInput").ap()
        io["onesc_in"] = nc.dram_tensor("onesc_in", [128, 1], F32, kind="ExternalInput").ap()
        io["ident_in"] = nc.dram_tensor("ident_in", [128, 128], F32, kind="ExternalInput").ap()
        io["out_fm"] = nc.dram_tensor("out_fm", [CH, T], F32, kind="ExternalOutput").ap()
        io["h_sp"] = nc.dram_tensor("h_sp", [INTER, T], BF16, kind="Internal").ap()
        p = Prog(nc)
        ctx = _ctx_setup(p, nc, io)
    else:
        nc, p, io, ctx = env
    emit_l1_ffn(nc, p, io, ctx)
    if env is None:
        p.close()
        return nc
    return None


def emit_l1_ffn(nc, p, io, ctx):
    x3_fm, wgT_in, wuT_in, wdT_in = io["x3_fm"], io["wgT_in"], io["wuT_in"], io["wdT_in"]
    out_fm, h_sp = io["out_fm"], io["h_sp"]
    wg_parts = io.get("wgT_parts")
    wu_parts = io.get("wuT_parts")
    wd_parts = io.get("wdT_parts")

    def _gu_ap(parts, whole, rs, re, c0):
        if parts is None:
            return whole[rs:re, c0:c0 + 128]
        t, o2 = divmod(c0, 512)
        return parts[t][rs:re, o2:o2 + 128]

    def _wd_aps(rs, re, c0, width):
        if wd_parts is None:
            return [(0, wdT_in[rs:re, c0:c0 + width])]
        res = []
        o2 = c0
        off = 0
        while off < width:
            t, rel = divmod(o2, 512)
            wnow = min(512 - rel, width - off)
            res.append((off, wd_parts[t][rs:re, rel:rel + wnow]))
            o2 += wnow
            off += wnow
        return res
    onescol, onescs_b = ctx["onescol"], ctx["onescs_b"]
    cbA = p.sb("cbA", 128, T)
    cbB = p.sb("cbB", 128, T)
    A2 = p.sb("A2", 128, NCH * T)

    es_f1 = contextlib.ExitStack()
    r_mean = p.sb("r_mean", 1, T, F32, es=es_f1, side="right")
    r_var = p.sb("r_var", 1, T, F32, es=es_f1, side="right")
    r_r1 = p.sb("r_r1", 1, T, F32, es=es_f1, side="right")
    r_tmp = p.sb("r_tmp", 1, T, F32, es=es_f1, side="right")
    with p.phase("ln") as (s, es):
        trb = [p.sb(f"h1t{i}", 128, T, es=es) for i in range(3)]
        tfq = [p.sb(f"h1q{i}", 128, T, es=es) for i in range(2)]
        ps_sum = p.psum("ps_sum", 1, T, es=es)
        ps_sq = p.psum("ps_sq", 1, T, es=es)
        for k in range(NCH):
            b = trb[k % 3]
            s.op("gpsimd",
                 (lambda kk, bb: lambda e: e.dma_start(
                     bb[:], x3_fm[kk * 128:(kk + 1) * 128, :]))(k, b),
                 dma=True, writes=[("tr", k % 3)])
            s.op("tensor",
                 (lambda kk, bb: lambda e: [
                     e.matmul(ps_sum[:, tc * FT:(tc + 1) * FT], onescs_b[:, :],
                              bb[:, tc * FT:(tc + 1) * FT],
                              start=(kk == 0), stop=(kk == NCH - 1))
                     for tc in range(NTC)][-1])(k, b),
                 reads=[("tr", k % 3), "onescs_b"], writes=["ps_sum"])
            sq = tfq[k % 2]
            s.op("scalar",
                 (lambda bb, qq: lambda e: e.activation(qq[:], bb[:], AF.Square))(b, sq),
                 reads=[("tr", k % 3)], writes=[("sq", k % 2)])
            s.op("tensor",
                 (lambda kk, qq: lambda e: [
                     e.matmul(ps_sq[:, tc * FT:(tc + 1) * FT], onescs_b[:, :],
                              qq[:, tc * FT:(tc + 1) * FT],
                              start=(kk == 0), stop=(kk == NCH - 1))
                     for tc in range(NTC)][-1])(k, sq),
                 reads=[("sq", k % 2), "onescs_b"], writes=["ps_sq"])
        s.op("vector", lambda e: e.tensor_copy(r_mean[:], ps_sum[:1, :]),
             reads=["ps_sum"], writes=["r_mean"])
        s.op("vector", lambda e: e.tensor_copy(r_var[:], ps_sq[:1, :]),
             reads=["ps_sq"], writes=["r_var"])

    with p.phase("ln2") as (s, es):
        trb = [p.sb(f"h2t{i}", 128, T, es=es) for i in range(3)]
        psb = [p.psum(f"psb{i}", 128, FT, es=es) for i in range(2)]
        s.op("vector", lambda e: e.tensor_scalar(r_mean[:], r_mean[:], 1.0 / CH, None, ALU.mult),
             reads=["r_mean"], writes=["r_mean"])
        s.op("vector", lambda e: e.tensor_scalar(r_var[:], r_var[:], 1.0 / CH, None, ALU.mult),
             reads=["r_var"], writes=["r_var"])
        s.op("vector", lambda e: e.tensor_tensor(r_tmp[:], r_mean[:], r_mean[:], ALU.mult),
             reads=["r_mean"], writes=["r_tmp"])
        s.op("vector", lambda e: e.tensor_tensor(r_var[:], r_var[:], r_tmp[:], ALU.subtract),
             reads=["r_tmp", "r_var"], writes=["r_var"])
        s.op("scalar", lambda e: e.activation(r_tmp[:], r_var[:], AF.Ln, bias=LN_EPS),
             reads=["r_var"], writes=["r_tmp"])
        s.op("scalar", lambda e: e.activation(r_r1[:], r_tmp[:], AF.Exp, scale=-0.5),
             reads=["r_tmp"], writes=["r_r1"])
        s.op("vector", lambda e: e.tensor_tensor(r_tmp[:], r_mean[:], r_r1[:], ALU.mult),
             reads=["r_mean", "r_r1"], writes=["r_tmp"])
        for tc in range(NTC):
            s.op("tensor",
                 (lambda t: lambda e: e.matmul(
                     psb[0][:], onescol[:1, :], r_r1[:, t * FT:(t + 1) * FT],
                     start=True, stop=True))(tc),
                 reads=["r_r1", "onescol"], writes=[("psb", 0)])
            s.op("vector",
                 (lambda t: lambda e: e.tensor_copy(cbA[:, t * FT:(t + 1) * FT], psb[0][:]))(tc),
                 reads=[("psb", 0)], writes=["cbA"])
            s.op("tensor",
                 (lambda t: lambda e: e.matmul(
                     psb[1][:], onescol[:1, :], r_tmp[:, t * FT:(t + 1) * FT],
                     start=True, stop=True))(tc),
                 reads=["r_tmp", "onescol"], writes=[("psb", 1)])
            s.op("vector",
                 (lambda t: lambda e: e.tensor_copy(cbB[:, t * FT:(t + 1) * FT], psb[1][:]))(tc),
                 reads=[("psb", 1)], writes=["cbB"])
        for k in range(NCH):
            b = trb[k % 3]
            s.op("gpsimd",
                 (lambda kk, bb: lambda e: e.dma_start(
                     bb[:], x3_fm[kk * 128:(kk + 1) * 128, :]))(k, b),
                 dma=True, writes=[("tr", k % 3)])
            s.op("vector",
                 (lambda bb: lambda e: e.tensor_tensor(bb[:], bb[:], cbA[:], ALU.mult))(b),
                 reads=[("tr", k % 3), "cbA"], writes=[("tr", k % 3)])
            s.op("vector",
                 (lambda kk, bb: lambda e: e.tensor_tensor(
                     A2[:, kk * T:(kk + 1) * T], bb[:], cbB[:], ALU.subtract))(k, b),
                 reads=[("tr", k % 3), "cbB"], writes=[("A2", k)])
    es_f1.close()

    with p.phase("gu") as (s, es):
        wqg = [p.sb(f"wqg{i}", 128, NCH * 128, es=es) for i in range(2)]
        wqu = [p.sb(f"wqu{i}", 128, NCH * 128, es=es) for i in range(2)]
        sgc = [p.sb(f"hsg{i}", 128, FT, es=es) for i in range(2)]
        hcb = [p.sb(f"hh{i}", 128, T, es=es) for i in range(2)]
        psu = [p.psum(f"hpu{i}", 128, FT, es=es) for i in range(2)]
        psg = [p.psum(f"hpg{i}", 128, FT, es=es) for i in range(2)]
        for io_c in range(NIC):
            r = io_c % 2
            s.op("gpsimd",
                 (lambda i2, bb: lambda e: [
                     e.dma_start(bb[:, k * 128:(k + 1) * 128],
                                 _gu_ap(wg_parts, wgT_in, k * 128, (k + 1) * 128, i2 * 128))
                     for k in range(NCH)])(io_c, wqg[r]),
                 dma=True, ndma=NCH, writes=[("wqg", r)])
            s.op("gpsimd",
                 (lambda i2, bb: lambda e: [
                     e.dma_start(bb[:, k * 128:(k + 1) * 128],
                                 _gu_ap(wu_parts, wuT_in, k * 128, (k + 1) * 128, i2 * 128))
                     for k in range(NCH)])(io_c, wqu[r]),
                 dma=True, ndma=NCH, writes=[("wqu", r)])
            for tc in range(NTC):
                s.op("tensor",
                     (lambda rr, t: lambda e: [
                         e.matmul(psg[t % 2][:], wqg[rr][:, k * 128:(k + 1) * 128],
                                  A2[:, k * T + t * FT:k * T + t * FT + FT],
                                  start=(k == 0), stop=(k == NCH - 1))
                         for k in range(NCH)][-1])(r, tc),
                     reads=[("wqg", r)], writes=[("psg", tc % 2)])
                s.op("tensor",
                     (lambda rr, t: lambda e: [
                         e.matmul(psu[t % 2][:], wqu[rr][:, k * 128:(k + 1) * 128],
                                  A2[:, k * T + t * FT:k * T + t * FT + FT],
                                  start=(k == 0), stop=(k == NCH - 1))
                         for k in range(NCH)][-1])(r, tc),
                     reads=[("wqu", r)], writes=[("psu", tc % 2)])
                s.op("scalar",
                     (lambda t, ss: lambda e: e.activation(ss[:], psg[t % 2][:], AF.Silu))(tc, sgc[tc % 2]),
                     reads=[("psg", tc % 2)], writes=[("sg", tc % 2)])
                s.op("vector",
                     (lambda t, ss, dd: lambda e: e.tensor_tensor(
                         dd[:, t * FT:(t + 1) * FT], psu[t % 2][:], ss[:], ALU.mult))(tc, sgc[tc % 2], hcb[r]),
                     reads=[("psu", tc % 2), ("sg", tc % 2)], writes=[("hcb", r)])
            s.op("sync",
                 (lambda i2, dd: lambda e: e.dma_start(
                     h_sp[i2 * 128:(i2 + 1) * 128, :], dd[:]))(io_c, hcb[r]),
                 dma=True, reads=[("hcb", r)])

    with p.phase("dn") as (s, es):
        hb = [p.sb(f"dh{i}", 128, FT, es=es) for i in range(2)]
        wb = [p.sb(f"dw{i}", 128, 1024, es=es) for i in range(2)]
        xb = [p.sb(f"dx{i}", 128, FT, F32, es=es) for i in range(2)]
        ob = [p.sb(f"do{i}", 128, FT, F32, es=es) for i in range(2)]
        ps8 = [p.psum(f"dp8_{i}", 128, FT, es=es) for i in range(8)]
        out_dmas = []
        for dgrp in range(2):
            for tc in range(NTC):
                for k in range(NIC):
                    r = k % 2
                    s.op("sync",
                         (lambda kk, t, bb: lambda e: e.dma_start(
                             bb[:], bass.AP(h_sp.tensor, (kk * 128) * T + t * FT,
                                            [[T, 128], [1, FT]])))(k, tc, hb[r]),
                         dma=True, writes=[("dh", r)])
                    nwd = len(_wd_aps(0, 128, dgrp * 1024, 1024))
                    s.op("sync",
                         (lambda kk, dg, bb: lambda e: [
                             e.dma_start(bb[:, doff:doff + ap.shape[1]], ap)
                             for doff, ap in _wd_aps(kk * 128, (kk + 1) * 128, dg * 1024, 1024)])(k, dgrp, wb[r]),
                         dma=True, ndma=nwd, writes=[("dw", r)])
                    for do in range(8):
                        s.op("tensor",
                             (lambda kk, d, rr: lambda e: e.matmul(
                                 ps8[d][:], wb[rr][:, d * 128:(d + 1) * 128], hb[rr][:],
                                 start=(kk == 0), stop=(kk == NIC - 1)))(k, do, r),
                             reads=[("dh", r), ("dw", r)], writes=[("dp8", do)])
                for do in range(8):
                    doch = dgrp * 8 + do
                    r = do % 2
                    s.op("sync",
                         (lambda dc, t, bb: lambda e: e.dma_start(
                             bb[:], x3_fm[dc * 128:(dc + 1) * 128, t * FT:(t + 1) * FT]))(doch, tc, xb[r]),
                         dma=True, writes=[("dx", r)])
                    s.op("vector",
                         (lambda d, bb, oo: lambda e: e.tensor_tensor(
                             oo[:], bb[:], ps8[d][:], ALU.add))(do, xb[r], ob[r]),
                         reads=[("dx", r), ("dp8", do)], writes=[("do", r)])
                    od = s.op("sync",
                              (lambda dc, t, oo: lambda e: e.dma_start(
                                  out_fm[dc * 128:(dc + 1) * 128, t * FT:(t + 1) * FT], oo[:]))(doch, tc, ob[r]),
                              dma=True, reads=[("do", r)])
                    out_dmas.append(od)
        s.op("gpsimd", lambda e: e.nop(), deps=out_dmas)


def run_l1_ffn(x3, wgT, wuT, wdT):
    ones = np.ones((1, 128), dtype=np.float32)
    ims = []
    for c in range(NCORE):
        b, h = _core_meta(c)
        xs = np.ascontiguousarray(x3[b, h * T:(h + 1) * T, :].T)
        ims.append(dict(x3_fm=xs, wgT_in=wgT, wuT_in=wuT, wdT_in=wdT, ones_in=ones))
    nc = build_l1_ffn()
    res = run_bass_kernel_spmd(nc, ims, core_ids=list(range(NCORE)))
    out = np.empty((4, 4096, CH), dtype=np.float32)
    for c in range(NCORE):
        b, h = _core_meta(c)
        out[b, h * T:(h + 1) * T, :] = res.results[c]["out_fm"].T
    return out


def _l1_pre_inmaps(x2, ins):
    ipT = np.ascontiguousarray(ins['L1_attn_in_proj_w'].T)
    xpT = np.ascontiguousarray(ins['L1_attn_x_proj_w'].T)
    dpT = np.ascontiguousarray(ins['L1_attn_dt_proj_w'].T)
    cw = ins['L1_attn_conv_w']
    cwc = np.ascontiguousarray(np.transpose(cw.reshape(NDI, 128, 4), (1, 0, 2)).reshape(128, NDI * 4))
    cbc = np.ascontiguousarray(ins['L1_attn_conv_b'].reshape(NDI, 128).T)
    dbc = np.ascontiguousarray(ins['L1_attn_dt_proj_b'].reshape(NDI, 128).T)
    ones = np.ones((1, 128), dtype=np.float32)
    onesc = np.ones((128, 1), dtype=np.float32)
    ims = []
    for c in range(NCORE):
        b, h = _core_meta(c)
        xs = np.ascontiguousarray(x2[b, h * T:(h + 1) * T, :].T)
        xprev = np.zeros((CH, 4), dtype=np.float32)
        if h == 1:
            xprev[:, 1:4] = x2[b, T - 3:T, :].T
        ims.append(dict(x2_fm=xs, ipT=ipT, xpT=xpT, dpT=dpT, cwc_in=cwc,
                        cbc_in=cbc, dbc_in=dbc, xprev_in=xprev,
                        ones_in=ones, onesc_in=onesc))
    return ims


def run_l1_pre(x2, ins):
    nc = build_l1_pre()
    ims = _l1_pre_inmaps(x2, ins)
    res = run_bass_kernel_spmd(nc, ims, core_ids=list(range(NCORE)))
    return res.results



WD_CNT = [6, 6, 6, 5, 5, 5, 5, 5]
WD_START = [0, 6, 12, 18, 23, 28, 33, 38]


def build_fused(upto=99):
    nc = bass.Bass("TRN2", target_bir_lowering=False)
    io = {}

    import os
    _MINSET = {"ones_in", "onesc_in", "ident_in", "coff_in", "zmask_in",
               "wgl0", "wul0", "wgc0", "wuc0", "wgl1", "wul1", "wgc1", "wuc1"}

    def din(nm, shape, dt=F32):
        if os.environ.get("KMINIO") and nm not in _MINSET:
            io[nm] = None
            return
        io[nm] = nc.dram_tensor(nm, shape, dt, kind="ExternalInput").ap()

    def dint(nm, shape, dt=F32, addr_space=None):
        if os.environ.get("KMINIO") and nm not in _MINSET:
            io[nm] = None
            return
        kw = dict(kind="Internal")
        if addr_space and not os.environ.get("KNOSHARED"):
            kw["addr_space"] = addr_space
        io[nm] = nc.dram_tensor(nm, shape, dt, **kw).ap()

    din("x_fm", [CH, T]); din("wqkvT", [CH, 3 * CH]); din("wswT", [CH, 16384])
    din("woutT", [8192, CH]); din("smask_in", [128, 256])
    din("rw_in", [CH, NEXP]); din("rb_in", [NEXP, 1]); din("ew_in", [CH, CH])
    din("ebc_in", [128, NCH])
    din("ipT", [CH, 2 * DI]); din("xpT", [DI, 160]); din("dpT", [128, DI])
    din("cwc_in", [128, NDI * 4]); din("cbc_in", [128, NDI]); din("dbc_in", [128, NDI])
    din("opT", [DI, CH])
    din("gidx", [256, INTER], I32); din("uidx", [256, INTER], I32)
    din("didx", [768, CH], I32); din("lutc_in", [128, 16])
    din("alog_in", [128, NCK * NST]); din("d_in", [128, NCK])
    din("ones_in", [1, 128]); din("onesc_in", [128, 1]); din("ident_in", [128, 128])
    din("coff_in", [1, 4], I32); din("zmask_in", [128, 1])
    io["out_fm"] = nc.dram_tensor("out_fm", [CH, T], F32, kind="ExternalOutput").ap()

    dint("x1_fm", [CH, T]); dint("x2_fm", [CH, T]); dint("x3_fm", [CH, T])
    dint("q_sp", [CH, T], BF16); dint("k_sp", [CH, T], BF16); dint("v_sp", [CH, T], BF16)
    dint("usg_sp", [8192, T], BF16); dint("wout_bf", [8192, CH], BF16)
    dint("rows_sp", [2, T]); dint("rows_sp2", [2, T]); dint("gsr_sp", [16, 128])
    dint("l0a_cc_in", [32, 128]); dint("l0a_cc_out", [NCORE * 32, 128])
    dint("l0f_cc_in", [128, 256]); dint("l0f_cc_out", [NCORE * 128, 256])
    dint("xc_sp", [DI, T], BF16); dint("up_sp", [DI, T], BF16)
    dint("u_loc", [DI, T], BF16); dint("dl_loc", [DI, T]); dint("sz_loc", [DI, T], BF16)
    dint("B_loc", [16, T]); dint("C_loc", [16, T])
    dint("op_bf", [DI, CH], BF16); dint("h_sp", [INTER, T], BF16)
    WG_CHUNKS = [512] * 10 + [384]
    for t, w in enumerate(WG_CHUNKS):
        dint(f"wgl{t}", [256, w], BF16)
        dint(f"wul{t}", [256, w], BF16)
    for t in range(4):
        dint(f"wdl{t}", [768, 512], BF16)
    for t, w in enumerate(WG_CHUNKS):
        dint(f"wgc{t}", [CH, w], BF16)
        dint(f"wuc{t}", [CH, w], BF16)
    for t in range(4):
        dint(f"wdag{t}", [NCORE * 768, 512], BF16)
        dint(f"wdc{t}", [INTER, 512], BF16)
    dint("sh_dl", [DI, LFULL], addr_space="Shared")
    dint("sh_u", [DI, LFULL], dt=BF16, addr_space="Shared")
    dint("sh_sz", [DI, LFULL], dt=BF16, addr_space="Shared")
    dint("sh_bc", [32, LFULL], addr_space="Shared")
    dint("sh_yg", [DI, LFULL], dt=BF16, addr_space="Shared")
    dint("sh_xp", [CH, 8], addr_space="Shared")
    dint("dls_loc", [CH, LFULL]); dint("us_loc", [CH, LFULL], BF16)
    dint("szs_loc", [CH, LFULL], BF16)
    dint("Bs_loc", [NST, LFULL]); dint("Cs_loc", [NST, LFULL])
    dint("ygs_loc", [CH, LFULL], BF16); dint("ygp_loc", [DI, T], BF16)

    p = Prog(nc)
    ctx = _ctx_setup(p, nc, io)
    zmask = p.sb("zmask", 128, 1, F32, es=p.es)
    coffs = p.sb("coffs", 1, 4, I32, es=p.es)
    regs = {}
    for eng in ("sync",):
        engobj = getattr(nc, eng)
        regs[eng] = [p.es.enter_context(engobj.register(f"r{eng}{i}")) for i in range(4)]
    env = (nc, p, io, ctx)

    import os
    with p.phase("fz") as (s, es):
        s.op("sync", lambda e: e.dma_start(zmask[:], io["zmask_in"]), dma=True, writes=["zmask"])
        s.op("sync", lambda e: e.dma_start(coffs[:], io["coff_in"]), dma=True, writes=["coffs"])
        if not os.environ.get("KNOREG"):
            def loadregs(e):
                last = None
                for i in range(4):
                    last = e.reg_load(regs["sync"][i], coffs[0:1, i:i + 1])
                return last
            s.op("sync", loadregs, reads=["coffs"], writes=["regs"])
    rA, rB, rC, rD = regs["sync"]

    p.prefix = "m_"
    es_m = contextlib.ExitStack(); p.cur_es = es_m
    if not os.environ.get("KSTUBMAT"):
        emit_l1_mat(nc, p, {**io,
                        "wg_out_parts": [io[f"wgl{t}"] for t in range(11)],
                        "wu_out_parts": [io[f"wul{t}"] for t in range(11)],
                        "wd_out_parts": [io[f"wdl{t}"] for t in range(4)],
                        "wg_chunk_cols": WG_CHUNKS, "wu_chunk_cols": WG_CHUNKS,
                        "wd_chunk_cols": [512] * 4,
                        "wg_out": None, "wu_out": None, "wd_out": None}, ctx)
    es_m.close(); p.cur_es = p.es
    if upto < 1:
        p.close()
        return nc
    p.barrier()
    with p.phase("wag") as (s, es):
        if not os.environ.get("KNOWAG"):
            nwag = int(os.environ.get("KWAGN", "99"))
            nemit = 0
            for t, w in enumerate(WG_CHUNKS):
                for src, dst in ((f"wgl{t}", f"wgc{t}"), (f"wul{t}", f"wuc{t}")):
                    if nemit >= nwag:
                        continue
                    nemit += 1
                    s.op("gpsimd",
                         (lambda sn, dn: lambda e: e.collective_compute(
                             "AllGather", ALU.bypass, ins=[io[sn].opt()],
                             outs=[io[dn].opt()],
                             replica_groups=[list(range(NCORE))]))(src, dst),
                         reads=[src], writes=[dst], coll=True)
            for t in range(4):
                if nemit >= nwag:
                    continue
                nemit += 1
                s.op("gpsimd",
                     (lambda t2: lambda e: e.collective_compute(
                         "AllGather", ALU.bypass,
                         ins=[io[f"wdl{t2}"].opt()],
                         outs=[io[f"wdag{t2}"].opt()],
                         replica_groups=[list(range(NCORE))]))(t),
                     reads=[f"wdl{t}"], writes=[f"wdag{t}"], coll=True)
        if not os.environ.get("KNOREPACK"):
            for t in range(4):
                s.op("sync",
                     (lambda t2: lambda e: [
                         e.dma_start(
                             io[f"wdc{t2}"][WD_START[r2] * 128:(WD_START[r2] + WD_CNT[r2]) * 128, :],
                             io[f"wdag{t2}"][768 * r2:768 * r2 + WD_CNT[r2] * 128, :])
                         for r2 in range(NCORE)])(t),
                     dma=True, ndma=NCORE, reads=[f"wdag{t}"] if not os.environ.get("KNOWAG") else [], writes=[f"wdc{t}"])

    if upto < 2:
        p.close()
        return nc
    p.prefix = "a_"
    es_a = contextlib.ExitStack(); p.cur_es = es_a
    emit_l0_attn(nc, p, io, ctx)
    es_a.close()
    if upto < 3:
        p.close()
        return nc
    p.barrier()
    p.prefix = "f_"
    es_f = contextlib.ExitStack(); p.cur_es = es_f
    emit_l0_ffn(nc, p, io, ctx)
    es_f.close(); p.cur_es = p.es

    if upto < 4:
        p.close()
        return nc
    p.barrier()
    xpt = p.sb("xpt", 128, NCH * 3, F32, es=p.es)
    with p.phase("xpc") as (s, es):
        s.op("sync", lambda e: e.dma_start(
            xpt[:].rearrange("p (k j) -> p k j", k=NCH),
            bass.AP(io["x2_fm"].tensor, T - 3, [[T, 128], [128 * T, NCH], [1, 3]])),
            dma=True, reads=["x2w"], writes=["xpt"])
        s.op("sync", lambda e: e.dma_start(
            bass.AP(io["sh_xp"].tensor, rC, [[8, 128], [8 * 128, NCH], [1, 3]]),
            xpt[:].rearrange("p (k j) -> p k j", k=NCH)),
            dma=True, reads=["xpt", "regs"], writes=["sh_xp"])
    p.barrier()

    if upto < 5:
        p.close()
        return nc
    p.prefix = "p_"
    es_p = contextlib.ExitStack(); p.cur_es = es_p
    xprev_view = bass.AP(io["sh_xp"].tensor, 0, [[8, CH], [1, 4]])
    emit_l1_pre(nc, p, {**io, "u_out": io["u_loc"], "dl_out": io["dl_loc"],
                        "sz_out": io["sz_loc"], "B_out": io["B_loc"],
                        "C_out": io["C_loc"]}, ctx,
                xprev_sb=(xprev_view, zmask))
    es_p.close(); p.cur_es = p.es
    if upto < 6:
        p.close()
        return nc
    p.barrier()

    with p.phase("xa") as (s, es):
        s.op("sync", lambda e: e.dma_start(
            bass.AP(io["sh_dl"].tensor, rA, [[LFULL, DI], [1, T]]), io["dl_loc"]),
            dma=True, reads=["dlw", "regs"], writes=["sh_dl"])
        s.op("sync", lambda e: e.dma_start(
            bass.AP(io["sh_u"].tensor, rA, [[LFULL, DI], [1, T]]), io["u_loc"]),
            dma=True, reads=["uw", "regs"], writes=["sh_u"])
        s.op("sync", lambda e: e.dma_start(
            bass.AP(io["sh_sz"].tensor, rA, [[LFULL, DI], [1, T]]), io["sz_loc"]),
            dma=True, reads=["szw", "regs"], writes=["sh_sz"])
        s.op("sync", lambda e: e.dma_start(
            bass.AP(io["sh_bc"].tensor, rA, [[LFULL, 16], [1, T]]), io["B_loc"]),
            dma=True, reads=["Bw", "regs"], writes=["sh_bc"])
        s.op("sync", lambda e: e.dma_start(
            bass.AP(io["sh_bc"].tensor, rD, [[LFULL, 16], [1, T]]), io["C_loc"]),
            dma=True, reads=["Cw", "regs"], writes=["sh_bc"])
    p.barrier()
    with p.phase("xar") as (s, es):
        s.op("sync", lambda e: e.dma_start(
            io["dls_loc"], bass.AP(io["sh_dl"].tensor, rB, [[LFULL, CH], [1, LFULL]])),
            dma=True, reads=["sh_dl", "regs"], writes=["dls"])
        s.op("sync", lambda e: e.dma_start(
            io["us_loc"], bass.AP(io["sh_u"].tensor, rB, [[LFULL, CH], [1, LFULL]])),
            dma=True, reads=["sh_u", "regs"], writes=["uss"])
        s.op("sync", lambda e: e.dma_start(
            io["szs_loc"], bass.AP(io["sh_sz"].tensor, rB, [[LFULL, CH], [1, LFULL]])),
            dma=True, reads=["sh_sz", "regs"], writes=["szs"])
        s.op("sync", lambda e: e.dma_start(io["Bs_loc"], io["sh_bc"][0:16, :]),
             dma=True, reads=["sh_bc"], writes=["Bs"])
        s.op("sync", lambda e: e.dma_start(io["Cs_loc"], io["sh_bc"][16:32, :]),
             dma=True, reads=["sh_bc"], writes=["Cs"])

    if upto < 7:
        p.close()
        return nc
    p.barrier()
    p.prefix = "s_"
    es_s = contextlib.ExitStack(); p.cur_es = es_s
    emit_l1_scan(nc, p, {**io, "dl_in": io["dls_loc"], "u_in": io["us_loc"],
                         "sz_in": io["szs_loc"], "B_in": io["Bs_loc"],
                         "C_in": io["Cs_loc"], "yg_out": io["ygs_loc"]}, ctx)
    es_s.close(); p.cur_es = p.es
    if upto < 8:
        p.close()
        return nc
    p.barrier()

    with p.phase("xb") as (s, es):
        s.op("sync", lambda e: e.dma_start(
            bass.AP(io["sh_yg"].tensor, rB, [[LFULL, CH], [1, LFULL]]), io["ygs_loc"]),
            dma=True, reads=["ygw", "regs"], writes=["sh_yg"])
    p.barrier()
    with p.phase("xbr") as (s, es):
        s.op("sync", lambda e: e.dma_start(
            io["ygp_loc"], bass.AP(io["sh_yg"].tensor, rA, [[LFULL, DI], [1, T]])),
            dma=True, reads=["sh_yg", "regs"], writes=["ygp"])

    if upto < 9:
        p.close()
        return nc
    p.barrier()
    p.prefix = "o_"
    es_o = contextlib.ExitStack(); p.cur_es = es_o
    emit_l1_post(nc, p, {**io, "yg_fm": io["ygp_loc"]}, ctx)
    if upto < 10:
        p.close()
        return nc
    es_o.close()
    p.barrier()
    p.prefix = "n_"
    es_n = contextlib.ExitStack(); p.cur_es = es_n
    emit_l1_ffn(nc, p, {**io,
                        "wgT_parts": [io[f"wgc{t}"] for t in range(11)],
                        "wuT_parts": [io[f"wuc{t}"] for t in range(11)],
                        "wdT_parts": [io[f"wdc{t}"] for t in range(4)],
                        "wgT_in": None, "wuT_in": None, "wdT_in": None}, ctx)
    es_n.close(); p.cur_es = p.es

    p.close()
    return nc


def _fused_inmaps(ins):
    x = np.asarray(ins['L0_attn_x'])
    wqkvT = np.ascontiguousarray(np.asarray(ins['L0_attn_w_qkv']).T)
    wswT = np.ascontiguousarray(np.asarray(ins['L0_attn_w_swiglu']).T)
    woutT = np.ascontiguousarray(np.asarray(ins['L0_attn_w_out']).T)
    rb = np.ascontiguousarray(np.asarray(ins['L0_ffn_router_b']).reshape(NEXP, 1))
    ebc = np.ascontiguousarray(np.asarray(ins['L0_ffn_expert_b']).reshape(NCH, 128).T)
    ipT = np.ascontiguousarray(np.asarray(ins['L1_attn_in_proj_w']).T)
    xpT = np.ascontiguousarray(np.asarray(ins['L1_attn_x_proj_w']).T)
    dpT = np.ascontiguousarray(np.asarray(ins['L1_attn_dt_proj_w']).T)
    cw = np.asarray(ins['L1_attn_conv_w'])
    cwc = np.ascontiguousarray(np.transpose(cw.reshape(NDI, 128, 4), (1, 0, 2)).reshape(128, NDI * 4))
    cbc = np.ascontiguousarray(np.asarray(ins['L1_attn_conv_b']).reshape(NDI, 128).T)
    dbc = np.ascontiguousarray(np.asarray(ins['L1_attn_dt_proj_b']).reshape(NDI, 128).T)
    opT = np.ascontiguousarray(np.asarray(ins['L1_attn_out_proj_w']).T)
    gidxT = np.ascontiguousarray(np.asarray(ins['L1_ffn_gate_idx']).T)
    uidxT = np.ascontiguousarray(np.asarray(ins['L1_ffn_up_idx']).T)
    didxT = np.ascontiguousarray(np.asarray(ins['L1_ffn_down_idx']).T)
    lutc = np.ascontiguousarray(np.tile(np.asarray(ins['L1_ffn_lut']), (128, 1))).astype(np.float32)
    alog = np.asarray(ins['L1_attn_A_log'])
    Dv = np.asarray(ins['L1_attn_D'])
    ones = np.ones((1, 128), dtype=np.float32)
    onesc = np.ones((128, 1), dtype=np.float32)
    ident = np.eye(128, dtype=np.float32)
    ims = []
    for c in range(NCORE):
        b, h = _core_meta(c)
        xs = np.ascontiguousarray(x[b, h * T:(h + 1) * T, :].T)
        mask = np.zeros(8, dtype=np.float32)
        if h == 1:
            mask[2 * b] = 1.0
        smask = np.tile(np.repeat(mask, 32), (128, 1)).astype(np.float32)
        ch0 = h * CH
        asl = alog[ch0:ch0 + CH].reshape(NCK, 128, NST)
        asl = np.ascontiguousarray(np.transpose(asl, (1, 0, 2)).reshape(128, NCK * NST))
        dsl = np.ascontiguousarray(Dv[ch0:ch0 + CH].reshape(NCK, 128).T)
        dtiles = np.zeros((768, CH), dtype=np.int32)
        st, cnt = WD_START[c], WD_CNT[c]
        dtiles[0:cnt * 128] = didxT[st * 128:(st + cnt) * 128]
        coff = np.array([[h * T, h * CH * LFULL, h * 4, 16 * LFULL + h * T]], dtype=np.int32)
        zm = np.full((128, 1), float(h), dtype=np.float32)
        ims.append(dict(
            x_fm=xs, wqkvT=wqkvT, wswT=wswT, woutT=woutT, smask_in=smask,
            rw_in=np.asarray(ins['L0_ffn_router_w']), rb_in=rb,
            ew_in=np.asarray(ins['L0_ffn_expert_w']), ebc_in=ebc,
            ipT=ipT, xpT=xpT, dpT=dpT, cwc_in=cwc, cbc_in=cbc, dbc_in=dbc,
            opT=opT, gidx=np.ascontiguousarray(gidxT[c * 256:(c + 1) * 256]),
            uidx=np.ascontiguousarray(uidxT[c * 256:(c + 1) * 256]),
            didx=dtiles, lutc_in=lutc, alog_in=asl, d_in=dsl,
            ones_in=ones, onesc_in=onesc, ident_in=ident,
            coff_in=coff, zmask_in=zm))
    return ims


def kernel(**inputs):
    ins = {k: np.asarray(v) for k, v in inputs.items()}
    nc = build_fused()
    ims = _fused_inmaps(ins)
    res = run_bass_kernel_spmd(nc, ims, core_ids=list(range(NCORE)))
    out = np.empty((4, 4096, CH), dtype=np.float32)
    for c in range(NCORE):
        b, h = _core_meta(c)
        out[b, h * T:(h + 1) * T, :] = res.results[c]["out_fm"].T
    return out


def kernel_unfused(**inputs):
    ins = inputs
    x = np.asarray(ins['L0_attn_x'])
    x1 = run_l0_attn(x, np.asarray(ins['L0_attn_w_qkv']),
                     np.asarray(ins['L0_attn_w_swiglu']), np.asarray(ins['L0_attn_w_out']))
    x2 = run_l0_ffn(x1, np.asarray(ins['L0_ffn_router_w']), np.asarray(ins['L0_ffn_router_b']),
                    np.asarray(ins['L0_ffn_expert_w']), np.asarray(ins['L0_ffn_expert_b']))
    insn = {k: np.asarray(v) for k, v in ins.items()}
    pre = run_l1_pre(x2, insn)
    sc = run_l1_scan(pre, insn)
    yg_full = np.empty((4, DI, 4096), dtype=sc[0]["yg_out"].dtype)
    for c in range(NCORE):
        b, hh = c // 2, c % 2
        yg_full[b][hh * CH:(hh + 1) * CH, :] = sc[c]["yg_out"]
    x3 = run_l1_post(x2, yg_full, insn)
    wgT, wuT, wdT = run_l1_mat(insn)
    out = run_l1_ffn(x3, wgT, wuT, wdT)
    return out
